# version 17
# baseline (speedup 1.0000x reference)
import os
import numpy as np
from contextlib import ExitStack

import concourse.bass as bass
import concourse.mybir as mybir
from concourse.bass_utils import run_bass_kernel_spmd

F32 = mybir.dt.float32
BF16 = mybir.dt.bfloat16
I32 = mybir.dt.int32
AF = mybir.ActivationFunctionType
ALU = mybir.AluOpType
AX = mybir.AxisListType

NCORES_G = 8
T = 1024
C = 2048
II = 8192
NT = 8
NCT = 16
NG = 64
EPS_LN = 1e-5
EPS_Q = 1e-5
INV_C = 1.0 / 2048.0
INV_128 = 1.0 / 128.0
INV_NW = 1.0 / 16777216.0


class Ctr:
    def __init__(self, sem):
        self.sem = sem
        self.n = 0


def inc(ctr, instr, k=1):
    ctr.n += k
    instr.then_inc(ctr.sem, k)


class _Stop(Exception):
    pass


def build_nc():
    nc = bass.Bass("TRN2", target_bir_lowering=False, detect_race_conditions=False)
    PHASE_LIMIT = int(os.environ.get("KPHASE", "99"))

    x_in = nc.dram_tensor("x", [T, C], F32, kind="ExternalInput").ap()
    w_at = {nm: nc.dram_tensor(nm, [C, C], F32, kind="ExternalInput").ap()
            for nm in ("wk", "wv", "wr", "wo")}
    wg = nc.dram_tensor("wg", [II, C], F32, kind="ExternalInput").ap()
    wu = nc.dram_tensor("wu", [II, C], F32, kind="ExternalInput").ap()
    wd = nc.dram_tensor("wd", [C, II], F32, kind="ExternalInput").ap()
    cv_in = {nm: nc.dram_tensor(nm, [128, NCT], F32, kind="ExternalInput").ap()
             for nm in ("tmk", "tmv", "tmr", "tfc", "xxc", "aac", "bbc", "ppc")}
    flag = nc.dram_tensor("flag", [1, 1], F32, kind="ExternalInput").ap()
    h128 = nc.dram_tensor("h128", [128, 128], F32, kind="ExternalInput").ap()
    h64b = nc.dram_tensor("h64b", [128, 128], F32, kind="ExternalInput").ap()

    o_out = nc.dram_tensor("out", [T, C], F32, kind="ExternalOutput").ap()
    _scr_kind = "ExternalOutput" if os.environ.get("KDUMP") else "Internal"
    z_hbm = nc.dram_tensor("z_scr", [128 * NG, T], BF16, kind=_scr_kind).ap()
    q2_hbm = nc.dram_tensor("q2_scr", [II, T], BF16, kind=_scr_kind).ap()
    x1_hbm = nc.dram_tensor("x1_scr", [T, C], F32, kind=_scr_kind).ap()
    dn_hbm = nc.dram_tensor("dn_scr", [C, T], BF16, kind=_scr_kind).ap()
    row_hbm = nc.dram_tensor("row_scr", [16, 128], F32, kind="Internal").ap()
    r3_hbm = nc.dram_tensor("r3_scr", [3, 128], F32, kind="Internal").ap()
    sc_hbm = nc.dram_tensor("sc_scr", [1, 12], F32, kind="Internal").ap()
    KDBG = bool(os.environ.get("KDBG"))
    if KDBG:
        dbg_q = nc.dram_tensor("dbg_q", [T, C], BF16, kind="Internal").ap()
        dbg_ao = nc.dram_tensor("dbg_ao", [128, NT * C], BF16, kind="ExternalOutput").ap()
        dbg_st = nc.dram_tensor("dbg_st", [128, 80], F32, kind="ExternalOutput").ap()
        dbg_vb = nc.dram_tensor("dbg_vb", [128, NCT * T], BF16, kind="ExternalOutput").ap()
        dbg_ln1 = nc.dram_tensor("dbg_ln1", [128, NCT * T], BF16, kind="ExternalOutput").ap()
        dbg_st1 = nc.dram_tensor("dbg_st1", [128, 80], F32, kind="ExternalOutput").ap()
        dbg_g = nc.dram_tensor("dbg_g", [128, T], F32, kind="Internal").ap()
        dbg_i = nc.dram_tensor("dbg_i", [128, T], F32, kind="Internal").ap()

    ctx = ExitStack()
    en = ctx.enter_context
    open_stacks = []

    def sbuf(name, shape, dt=F32):
        return en(nc.sbuf_tensor(name, shape, dt))

    s_cv = {nm: sbuf("s_" + nm, [128, NCT]) for nm in cv_in}
    s_off = sbuf("s_off", [128, 3 * NCT])
    s_flag = sbuf("s_flag", [128, 1])
    s_h128 = sbuf("s_h128", [128, 128], BF16)
    s_h64b = sbuf("s_h64b", [128, 128], BF16)
    s_stat = sbuf("s_stat", [128, 80])
    s_sclp = sbuf("s_sclp", [128, 192])
    s_wrow = sbuf("s_wrow", [1, 384])
    s_wsc = sbuf("s_wsc", [1, 12])
    s_wscb = sbuf("s_wscb", [128, 12])
    s_qcol = sbuf("s_qcol", [128, 64])
    s_wkv = sbuf("s_wkv", [128, 12 * NCT])
    s_xk0 = sbuf("s_xk0", [128, NCT], BF16)
    s_v0 = sbuf("s_v0", [128, NCT])
    s_r0 = sbuf("s_r0", [128, NCT])

    xw = [sbuf(f"xw{i}", [128, C]) for i in range(2)]
    x1s = [sbuf(f"x1s{i}", [128, C]) for i in range(2)]

    p_a = en(nc.psum_tensor("p_a", [128, 1024], F32))
    p_b = en(nc.psum_tensor("p_b", [128, 1024], F32))
    p_z = en(nc.psum_tensor("p_z", [128, 1024], F32))
    p_z2 = en(nc.psum_tensor("p_z2", [128, 1024], F32))

    Sm = lambda nm: Ctr(en(nc.semaphore(nm)))
    m_cst = Sm("m_cst")
    m_cst2 = Sm("m_cst2")
    m_ld = [Sm(f"m_ld{i}") for i in range(2)]
    m_w = [Sm(f"m_w{i}") for i in range(2)]
    m_scl = [Sm(f"m_scl{i}") for i in range(2)]
    m_fw = [Sm(f"m_fw{i}") for i in range(2)]
    m_z = [Sm(f"m_z{i}") for i in range(2)]
    m_zw = Sm("m_zw")
    m_q2w = Sm("m_q2w")
    m_q2r = Sm("m_q2r")
    m_x1w = Sm("m_x1w")
    m_dnw = Sm("m_dnw")
    m_row = Sm("m_row")
    m_tr = Sm("m_tr")
    m_out = Sm("m_out")
    m_dbg = Sm("m_dbg")
    c_t = Sm("c_t")
    c_v = Sm("c_v")
    c_s = Sm("c_s")
    c_g = Sm("c_g")
    scl_done = [Sm(f"scl_done{i}") for i in range(2)]

    block = en(nc.Block())
    Sy = block.sync
    V = block.vector
    Sc = block.scalar
    Tn = block.tensor
    G = block.gpsimd

    def vstep(v, instr):
        inc(c_v, instr)
        v.wait_ge(c_v.sem, c_v.n)

    last_tr = [None]

    def issue_transpose(s, ctr, out, in_):
        if last_tr[0] is not None:
            psem, pcnt = last_tr[0]
            s.wait_ge(psem, pcnt)
        inc(ctr, s.dma_start(out=out, in_=in_, transpose=True), 16)
        last_tr[0] = (ctr.sem, ctr.n)

    def join():
        pass

    def maybe_stop(pid):
        if PHASE_LIMIT <= pid:
            @Sy
            def _(q):
                q.wait_ge(c_v.sem, c_v.n)
                q.wait_ge(c_s.sem, c_s.n)
                q.wait_ge(c_t.sem, c_t.n)
                for j in range(NT):
                    inc(m_out, q.dma_start(out=o_out[j * 128:(j + 1) * 128, :],
                                           in_=x1s[j % 2][:]), 16)
                q.wait_ge(m_out.sem, m_out.n)
            raise _Stop()

    try:
        def ln_math(eps):
            centry_v, centry_s = c_v.n, c_s.n

            @V
            def _(v):
                mean = s_stat[:, 16:24]
                t0 = s_stat[:, 32:40]
                t1 = s_stat[:, 40:48]
                v.wait_ge(c_v.sem, centry_v)
                v.wait_ge(c_s.sem, centry_s)
                vstep(v, v.tensor_scalar(out=mean, in0=s_stat[:, 0:8], scalar1=INV_C,
                                scalar2=None, op0=ALU.mult))
                vstep(v, v.tensor_scalar(out=t0, in0=s_stat[:, 8:16], scalar1=INV_C,
                                scalar2=None, op0=ALU.mult))
                vstep(v, v.tensor_tensor(out=t1, in0=mean, in1=mean, op=ALU.mult))
                vstep(v, v.tensor_tensor(out=t0, in0=t0, in1=t1, op=ALU.subtract))
                vstep(v, v.tensor_scalar(out=t0, in0=t0, scalar1=eps, scalar2=None, op0=ALU.add))
                inc(c_v, v.reciprocal(out=t1, in_=t0))
            vn = c_v.n

            @Sc
            def _(s):
                s.wait_ge(c_v.sem, vn)
                inc(c_s, s.activation(out=s_stat[:, 48:56], in_=s_stat[:, 40:48],
                                      func=AF.Sqrt))
            sn = c_s.n

            @V
            def _(v):
                rs = s_stat[:, 24:32]
                t0 = s_stat[:, 32:40]
                t1 = s_stat[:, 40:48]
                rs0 = s_stat[:, 48:56]
                v.wait_ge(c_s.sem, sn)
                vstep(v, v.tensor_tensor(out=t1, in0=rs0, in1=rs0, op=ALU.mult))
                vstep(v, v.tensor_tensor(out=t1, in0=t1, in1=t0, op=ALU.mult))
                vstep(v, v.tensor_scalar(out=t1, in0=t1, scalar1=-0.5, scalar2=1.5,
                                op0=ALU.mult, op1=ALU.add))
                inc(c_v, v.tensor_tensor(out=rs, in0=rs0, in1=t1, op=ALU.mult))

        @Sy
        def _(q):
            for nm in cv_in:
                inc(m_cst, q.dma_start(out=s_cv[nm][:], in_=cv_in[nm]), 16)
            inc(m_cst, q.dma_start(out=s_flag[:],
                                   in_=flag.partition_broadcast(128)[:, 0, :]), 16)

        @G
        def _(q):
            inc(m_cst2, q.dma_start(out=s_h128[:], in_=h128), 16)
            inc(m_cst2, q.dma_start(out=s_h64b[:], in_=h64b), 16)
        cst_n = m_cst.n
        cst2_n = m_cst2.n

        @Sc
        def _(s):
            s.preload_act_table(AF.Sqrt)

        @V
        def _(v):
            v.wait_ge(m_cst.sem, cst_n)
            v.wait_ge(m_cst2.sem, cst2_n)
            v.memset(s_wsc[:], 0.0)
            v.memset(s_stat[:], 0.0)
            v.memset(s_qcol[:], 0.0)
            v.memset(s_wkv[:], 0.0)
            for idx, nm in enumerate(("tmk", "tmv", "tmr")):
                o = s_off[:, idx * NCT:(idx + 1) * NCT]
                vstep(v, v.tensor_scalar(out=o, in0=s_cv[nm][:], scalar1=-1.0, scalar2=1.0,
                                op0=ALU.mult, op1=ALU.add))
                inc(c_v, v.tensor_tensor(out=o, in0=o, in1=s_cv["xxc"][:], op=ALU.mult))
        off_done = c_v.n

        SCT = 128 * 2048
        wg_f = wg.rearrange("a b -> (a b)")
        wu_f = wu.rearrange("a b -> (a b)")
        wd_f = wd.rearrange("a b -> (a b)")
        scale_tiles = (
            [(wg_f[i * SCT:(i + 1) * SCT].rearrange("(p f) -> p f", p=128), "v", 0, i)
             for i in range(64)]
            + [(wu_f[i * SCT:(i + 1) * SCT].rearrange("(p f) -> p f", p=128), "s", 1, i)
               for i in range(64)]
            + [(wd_f[i * SCT:(i + 1) * SCT].rearrange("(p f) -> p f", p=128),
                "v" if i % 2 == 0 else "s", 2, i) for i in range(64)]
        )

        actx = ExitStack()
        open_stacks.append(actx)
        ae = actx.enter_context
        dump = ae(nc.sbuf_tensor("dumpA", [128, 2048], F32))
        scl = [ae(nc.sbuf_tensor(f"scl{i}", [128, 2048], F32)) for i in range(2)]
        wstrA = [ae(nc.sbuf_tensor(f"wstrA{i}", [128, C], BF16)) for i in range(2)]
        ln1_tok = [ae(nc.sbuf_tensor(f"ln1tok{i}", [128, C], BF16)) for i in range(2)]
        vbf = ae(nc.sbuf_tensor("vbf", [128, NCT, T], BF16))
        rt = [ae(nc.sbuf_tensor(f"rt{i}", [128, T], BF16)) for i in range(2)]
        s_k0r = ae(nc.sbuf_tensor("s_k0r", [1, C], F32))

        scl_cons = [0, 0]
        scale_iter = iter(enumerate(scale_tiles))

        def emit_scale(nmax):
            for _ in range(nmax):
                try:
                    i, (src, eng, col, sub) = next(scale_iter)
                except StopIteration:
                    return
                sl = i % 2

                @Sy
                def _(q, i=i, sl=sl, src=src):
                    if i >= 2:
                        q.wait_ge(scl_done[sl].sem, scl_cons[sl])
                    inc(m_scl[sl], q.dma_start(out=scl[sl][:], in_=src), 16)
                ldn = m_scl[sl].n

                if eng == "v":
                    @V
                    def _(v, i=i, sl=sl, ldn=ldn):
                        v.wait_ge(m_scl[sl].sem, ldn)
                        inc(scl_done[sl], v.reduce_sum(out=s_sclp[:, i:i + 1], in_=scl[sl][:],
                                                       axis=AX.X, apply_absolute_value=True))
                else:
                    @Sc
                    def _(s, i=i, sl=sl, ldn=ldn):
                        s.wait_ge(m_scl[sl].sem, ldn)
                        inc(scl_done[sl], s.activation(out=dump[:], in_=scl[sl][:], func=AF.Abs,
                                                       accum_out=s_sclp[:, i:i + 1]))
                scl_cons[sl] += 1


        for j in range(NT):
            sl = j % 2
            emit_scale(2)

            @Sy
            def _(q, j=j, sl=sl):
                if j >= 2:
                    q.wait_ge(c_s.sem, c_s.n)
                    q.wait_ge(c_v.sem, c_v.n)
                inc(m_ld[sl], q.dma_start(out=xw[sl][:], in_=x_in[j * 128:(j + 1) * 128, :]), 16)
            ldn = m_ld[sl].n

            @Sc
            def _(s, j=j, sl=sl, ldn=ldn):
                s.wait_ge(m_ld[sl].sem, ldn)
                inc(c_s, s.activation(out=dump[:, 0:C], in_=xw[sl][:], func=AF.Square,
                                      accum_out=s_stat[:, 8 + j:9 + j]))

            @V
            def _(v, j=j, sl=sl, ldn=ldn):
                v.wait_ge(m_ld[sl].sem, ldn)
                inc(c_v, v.reduce_sum(out=s_stat[:, j:j + 1], in_=xw[sl][:], axis=AX.X))

        ln_math(EPS_LN)
        ln1_rs = c_v.n

        lctx = ExitStack()
        open_stacks.append(lctx)
        ln1T = lctx.enter_context(nc.sbuf_tensor("ln1T", [128, NCT, T], BF16))
        xT = lctx.enter_context(nc.sbuf_tensor("xT", [128, NCT, T], BF16))

        for j in range(NT):
            sl = j % 2

            @Sy
            def _(q, j=j, sl=sl):
                q.wait_ge(c_v.sem, c_v.n)
                inc(m_ld[sl], q.dma_start(out=xw[sl][:], in_=x_in[j * 128:(j + 1) * 128, :]), 16)
            ldn = m_ld[sl].n
            mtr_all = m_tr.n

            @V
            def _(v, j=j, sl=sl, ldn=ldn, mtr_all=mtr_all):
                v.wait_ge(m_ld[sl].sem, ldn)
                if j >= 2:
                    v.wait_ge(m_tr.sem, mtr_all)
                inc(c_v, v.tensor_scalar(
                    out=ln1_tok[sl][:], in0=xw[sl][:],
                    scalar1=s_stat[:, 16 + j:17 + j], scalar2=s_stat[:, 24 + j:25 + j],
                    op0=ALU.subtract, op1=ALU.mult))
            nn_ = c_v.n

            @Sc
            def _(s, j=j, sl=sl, nn_=nn_):
                s.wait_ge(c_v.sem, nn_)
                issue_transpose(s, m_tr, ln1T[:, :, j * 128:(j + 1) * 128],
                                ln1_tok[sl].ap().rearrange("p (k f) -> p k f", k=NCT))
        ln1T_done = m_tr.n
        if KDBG:
            @G
            def _(q):
                q.wait_ge(m_tr.sem, ln1T_done)
                inc(m_dbg, q.dma_start(out=dbg_ln1, in_=ln1T.ap().rearrange("p a b -> p (a b)")), 16)
                inc(m_dbg, q.dma_start(out=dbg_st1, in_=s_stat[:]), 16)
        maybe_stop(1)

        def scale_finalize():
            @V
            def _(v):
                v.wait_ge(scl_done[0].sem, scl_cons[0])
                v.wait_ge(scl_done[1].sem, scl_cons[1])
                v.reduce_sum(out=s_stat[:, 60:61], in_=s_sclp[:, 0:64], axis=AX.X)
                v.reduce_sum(out=s_stat[:, 61:62], in_=s_sclp[:, 64:128], axis=AX.X)
                vstep(v, v.reduce_sum(out=s_stat[:, 62:63], in_=s_sclp[:, 128:192], axis=AX.X))
                inc(c_v, v.nop())
            vn = c_v.n

            @Sy
            def _(q):
                q.wait_ge(c_v.sem, vn)
                inc(m_row, q.dma_start(out=r3_hbm.rearrange("w p -> p w"), in_=s_stat[:, 60:63], allow_slow_non_contiguous=True), 16)
                q.wait_ge(m_row.sem, m_row.n)
                inc(m_row, q.dma_start(out=s_wrow[:], in_=r3_hbm.rearrange("w p -> (w p)").rearrange("(a b) -> a b", a=1)), 16)
            rown = m_row.n

            @V
            def _(v):
                v.wait_ge(m_row.sem, rown)
                for w in range(3):
                    vstep(v, v.reduce_sum(out=s_wsc[0:1, w:w + 1], in_=s_wrow[0:1, w * 128:(w + 1) * 128],
                                 axis=AX.X))
                vstep(v, v.tensor_scalar(out=s_wsc[0:1, 3:6], in0=s_wsc[0:1, 0:3], scalar1=INV_NW,
                                scalar2=EPS_Q, op0=ALU.mult, op1=ALU.max))
                vstep(v, v.tensor_scalar(out=s_wsc[0:1, 0:3], in0=s_wsc[0:1, 3:6], scalar1=0.5,
                                scalar2=None, op0=ALU.mult))
                inc(c_v, v.tensor_scalar(out=s_wsc[0:1, 6:9], in0=s_wsc[0:1, 0:3], scalar1=-1.0,
                                scalar2=None, op0=ALU.mult))
            vn = c_v.n

            @Sy
            def _(q):
                q.wait_ge(c_v.sem, vn)
                inc(m_row, q.dma_start(out=sc_hbm, in_=s_wsc[:]), 16)
                q.wait_ge(m_row.sem, m_row.n)
                inc(m_row, q.dma_start(out=s_wscb[:], in_=sc_hbm.partition_broadcast(128)[:, 0, :]), 16)
            return m_row.n

        def affine_into_xT(tm_nm, off_idx, twait):
            @V
            def _(v):
                v.wait_ge(m_tr.sem, ln1T_done)
                v.wait_ge(c_v.sem, off_done)
                v.wait_ge(c_t.sem, twait)
                last = None
                for ct in range(NCT):
                    last = v.tensor_scalar(
                        out=xT[:, ct, :], in0=ln1T[:, ct, :],
                        scalar1=s_cv[tm_nm][:, ct:ct + 1],
                        scalar2=s_off[:, off_idx * NCT + ct:off_idx * NCT + ct + 1],
                        op0=ALU.mult, op1=ALU.add)
                inc(c_v, last)
            return c_v.n

        wslot_free = [[0], [0]]
        pg_free = [[0, "s"], [0, "s"]]

        def attn_pass(wsrc, mode, rhs_buf, rhs_ready_vn):
            for g in range(NCT):
                sl = g % 2
                emit_scale(3)

                @G
                def _(q, g=g, sl=sl):
                    q.wait_ge(c_t.sem, wslot_free[sl][0])
                    inc(m_w[sl], q.dma_start(out=wstrA[sl][:],
                                             in_=wsrc[g * 128:(g + 1) * 128, :]), 16)
                wn = m_w[sl].n

                if mode == "k0":
                    k0_pv = c_v.n

                    @Tn
                    def _(t, g=g, sl=sl, wn=wn, k0_pv=k0_pv):
                        t.wait_ge(m_w[sl].sem, wn)
                        t.wait_ge(c_v.sem, rhs_ready_vn)
                        if g > 0:
                            t.wait_ge(c_v.sem, k0_pv)
                        last = None
                        for ct in range(NCT):
                            last = t.matmul(p_z[0:1, 0:128], s_xk0[:, ct:ct + 1],
                                            wstrA[sl][:, ct * 128:(ct + 1) * 128],
                                            start=(ct == 0), stop=(ct == NCT - 1))
                        inc(c_t, last)
                    wslot_free[sl][0] = c_t.n
                    tcn = c_t.n

                    @V
                    def _(v, g=g, tcn=tcn):
                        v.wait_ge(c_t.sem, tcn)
                        inc(c_v, v.tensor_copy(out=s_k0r[0:1, g * 128:(g + 1) * 128],
                                               in_=p_z[0:1, 0:128]))
                    continue

                pg = p_a if g % 2 == 0 else p_b
                evprev, evkind = pg_free[g % 2]

                @Tn
                def _(t, g=g, sl=sl, wn=wn, pg=pg, evprev=evprev, evkind=evkind):
                    t.wait_ge(m_w[sl].sem, wn)
                    t.wait_ge(c_v.sem, rhs_ready_vn)
                    if evprev > 0:
                        t.wait_ge((c_s if evkind == "s" else c_v).sem, evprev)
                    last = None
                    for ct in range(NCT):
                        for th in range(2):
                            last = t.matmul(pg[:, th * 512:(th + 1) * 512],
                                            wstrA[sl][:, ct * 128:(ct + 1) * 128],
                                            rhs_buf[:, ct, th * 512:(th + 1) * 512],
                                            start=(ct == 0), stop=(ct == NCT - 1))
                    inc(c_t, last)
                wslot_free[sl][0] = c_t.n
                tcn = c_t.n

                if mode == "v":
                    @Sc
                    def _(s, g=g, pg=pg, tcn=tcn):
                        s.wait_ge(c_t.sem, tcn)
                        inc(c_s, s.activation(out=vbf[:, g, :], in_=pg[:], func=AF.Copy))
                    pg_free[g % 2] = [c_s.n, "s"]
                elif mode == "r":
                    @Sc
                    def _(s, g=g, sl=sl, pg=pg, tcn=tcn):
                        s.wait_ge(c_t.sem, tcn)
                        if g >= 2:
                            s.wait_ge(c_v.sem, c_v.n)
                        inc(c_s, s.activation(out=rt[sl][:], in_=pg[:], func=AF.Sigmoid))
                    pg_free[g % 2] = [c_s.n, "s"]
                    scn = c_s.n

                    @V
                    def _(v, g=g, sl=sl, scn=scn):
                        v.wait_ge(c_s.sem, scn)
                        v.tensor_copy(out=s_r0[:, g:g + 1], in_=rt[sl][:, 0:1])
                        inc(c_v, v.tensor_tensor(out=vbf[:, g, :], in0=rt[sl][:, :],
                                                 in1=vbf[:, g, :], op=ALU.mult))
                else:
                    @Sc
                    def _(s, g=g, pg=pg, tcn=tcn):
                        s.wait_ge(c_t.sem, tcn)
                        inc(c_s, s.activation(out=aobf[:, g, :], in_=pg[:], func=AF.Copy))
                    pg_free[g % 2] = [c_s.n, "s"]

        xv_n = affine_into_xT("tmv", 1, 0)
        attn_pass(w_at["wv"], "v", xT, xv_n)
        v_done_s = c_s.n

        @V
        def _(v):
            v.wait_ge(c_s.sem, v_done_s)
            inc(c_v, v.tensor_copy(out=s_v0[:],
                                   in_=vbf[:, :, 0:1].rearrange("p a b -> p (a b)")))
        v0_n = c_v.n

        @V
        def _(v):
            v.wait_ge(m_tr.sem, ln1T_done)
            v.wait_ge(c_v.sem, off_done)
            t = s_wkv[:, 0:NCT]
            vstep(v, v.tensor_tensor(out=t, in0=ln1T[:, :, 0:1].rearrange("p a b -> p (a b)"),
                            in1=s_cv["tmk"][:], op=ALU.mult))
            inc(c_v, v.tensor_tensor(out=s_xk0[:], in0=t, in1=s_off[:, 0:NCT], op=ALU.add))
        xk0_n = c_v.n

        @Sc
        def _(s):
            s.preload_act_table(AF.Sigmoid)

        xr_n = affine_into_xT("tmr", 2, c_t.n)
        attn_pass(w_at["wr"], "r", xT, xr_n)
        r_done_v = c_v.n

        attn_pass(w_at["wk"], "k0", None, xk0_n)
        k0_done_v = c_v.n

        @Sy
        def _(q):
            q.wait_ge(c_v.sem, k0_done_v)
            inc(m_row, q.dma_start(out=row_hbm.rearrange("a b -> (a b)").rearrange("(a b) -> a b", a=1), in_=s_k0r[:]), 16)
            q.wait_ge(m_row.sem, m_row.n)
            inc(m_row, q.dma_start(out=s_wkv[:, NCT:2 * NCT], in_=row_hbm.rearrange("a b -> b a"), allow_slow_non_contiguous=True), 16)
        k0c_n = m_row.n

        @V
        def _(v):
            v.wait_ge(m_row.sem, k0c_n)
            v.wait_ge(c_v.sem, r_done_v)
            k0c = s_wkv[:, NCT:2 * NCT]
            ww = s_wkv[:, 2 * NCT:3 * NCT]
            qq = s_wkv[:, 3 * NCT:4 * NCT]
            d1 = s_wkv[:, 4 * NCT:5 * NCT]
            d2 = s_wkv[:, 5 * NCT:6 * NCT]
            vstep(v, v.tensor_tensor(out=ww, in0=s_cv["tfc"][:], in1=k0c, op=ALU.add))
            vstep(v, v.tensor_tensor(out=qq, in0=s_cv["ppc"][:], in1=ww, op=ALU.max))
            vstep(v, v.tensor_tensor(out=d1, in0=s_cv["ppc"][:], in1=qq, op=ALU.subtract))
            inc(c_v, v.tensor_tensor(out=d2, in0=ww, in1=qq, op=ALU.subtract))
        vn = c_v.n

        @Sc
        def _(s):
            s.preload_act_table(AF.Exp)
            s.wait_ge(c_v.sem, vn)
            s.activation(out=s_wkv[:, 6 * NCT:7 * NCT], in_=s_wkv[:, 4 * NCT:5 * NCT], func=AF.Exp)
            inc(c_s, s.activation(out=s_wkv[:, 7 * NCT:8 * NCT], in_=s_wkv[:, 5 * NCT:6 * NCT],
                                  func=AF.Exp))
        sn = c_s.n

        @V
        def _(v):
            v.wait_ge(c_s.sem, sn)
            e1 = s_wkv[:, 6 * NCT:7 * NCT]
            e2 = s_wkv[:, 7 * NCT:8 * NCT]
            num = s_wkv[:, 8 * NCT:9 * NCT]
            den = s_wkv[:, 9 * NCT:10 * NCT]
            tt = s_wkv[:, 10 * NCT:11 * NCT]
            sel = s_wkv[:, 11 * NCT:12 * NCT]
            vstep(v, v.tensor_tensor(out=num, in0=e1, in1=s_cv["aac"][:], op=ALU.mult))
            vstep(v, v.tensor_tensor(out=tt, in0=e2, in1=s_v0[:], op=ALU.mult))
            vstep(v, v.tensor_tensor(out=num, in0=num, in1=tt, op=ALU.add))
            vstep(v, v.tensor_tensor(out=den, in0=e1, in1=s_cv["bbc"][:], op=ALU.mult))
            vstep(v, v.tensor_tensor(out=den, in0=den, in1=e2, op=ALU.add))
            vstep(v, v.reciprocal(out=den, in_=den))
            vstep(v, v.tensor_tensor(out=tt, in0=num, in1=den, op=ALU.mult))
            vstep(v, v.tensor_tensor(out=tt, in0=tt, in1=s_v0[:], op=ALU.subtract))
            vstep(v, v.tensor_scalar(out=tt, in0=tt, scalar1=s_flag[:, 0:1], scalar2=None, op0=ALU.mult))
            vstep(v, v.tensor_tensor(out=sel, in0=s_v0[:], in1=tt, op=ALU.add))
            vstep(v, v.tensor_tensor(out=sel, in0=s_r0[:], in1=sel, op=ALU.mult))
            inc(c_v, v.tensor_copy(out=vbf[:, :, 0:1].rearrange("p a b -> p (a b)"), in_=sel))
        a0_n = c_v.n
        maybe_stop(3)

        join()
        lctx.close(); open_stacks.remove(lctx)
        a2ctx = ExitStack()
        open_stacks.append(a2ctx)
        aobf = a2ctx.enter_context(nc.sbuf_tensor("aobf", [128, NCT, T], BF16))
        ao_tok = a2ctx.enter_context(nc.sbuf_tensor("ao_tok", [128, NT, C], BF16))
        maybe_stop(30)

        if KDBG:
            @G
            def _(q):
                q.wait_ge(c_v.sem, a0_n)
                inc(m_dbg, q.dma_start(out=dbg_vb, in_=vbf.ap().rearrange("p a b -> p (a b)")), 16)
        attn_pass(w_at["wo"], "o", vbf, a0_n)
        ao_done_s = c_s.n
        emit_scale(999)
        wscb_done = scale_finalize()
        maybe_stop(31)

        @Sc
        def _(s):
            s.wait_ge(c_s.sem, ao_done_s)
            for m in range(NCT):
                issue_transpose(s, m_tr, ao_tok[:, :, m * 128:(m + 1) * 128],
                                aobf[:, m, :].rearrange("p (a b) -> p a b", a=NT))
        aot_n = m_tr.n
        maybe_stop(32)

        @Sc
        def _(s):
            s.preload_act_table(AF.Sqrt)

        for j in range(NT):
            @Sc
            def _(s, j=j):
                s.wait_ge(m_tr.sem, aot_n)
                inc(c_s, s.activation(out=dump[:, 0:C], in_=ao_tok[:, j, :], func=AF.Square,
                                      accum_out=s_stat[:, 8 + j:9 + j]))

            @V
            def _(v, j=j):
                v.wait_ge(m_tr.sem, aot_n)
                inc(c_v, v.reduce_sum(out=s_stat[:, j:j + 1], in_=ao_tok[:, j, :], axis=AX.X))

        ln_math(EPS_LN)
        lna_rs = c_v.n
        if KDBG:
            @G
            def _(q):
                q.wait_ge(c_v.sem, lna_rs)
                inc(m_dbg, q.dma_start(out=dbg_ao, in_=ao_tok.ap().rearrange("p a b -> p (a b)")), 16)
                inc(m_dbg, q.dma_start(out=dbg_st, in_=s_stat[:]), 16)
        maybe_stop(33)

        for j in range(NT):
            sl = j % 2

            @Sy
            def _(q, j=j, sl=sl):
                q.wait_ge(c_v.sem, c_v.n)
                if j >= 2:
                    q.wait_ge(m_x1w.sem, m_x1w.n)
                inc(m_ld[sl], q.dma_start(out=xw[sl][:], in_=x_in[j * 128:(j + 1) * 128, :]), 16)
            ldn = m_ld[sl].n

            @V
            def _(v, j=j, sl=sl, ldn=ldn):
                v.wait_ge(m_ld[sl].sem, ldn)
                v.wait_ge(c_v.sem, lna_rs)
                vstep(v, v.tensor_scalar(out=x1s[sl][:], in0=ao_tok[:, j, :],
                                scalar1=s_stat[:, 16 + j:17 + j],
                                scalar2=s_stat[:, 24 + j:25 + j],
                                op0=ALU.subtract, op1=ALU.mult))
                vstep(v, v.tensor_tensor(out=x1s[sl][:], in0=x1s[sl][:], in1=xw[sl][:], op=ALU.add))
                v.reduce_sum(out=s_qcol[:, 48 + j:49 + j], in_=x1s[sl][:], axis=AX.X)
                v.reduce_max(out=s_qcol[:, j:j + 1], in_=x1s[sl][:], axis=AX.X)
                inc(c_v, v.tensor_reduce(out=s_qcol[:, 8 + j:9 + j], in_=x1s[sl][:],
                                         axis=AX.X, op=ALU.min))
            vvn = c_v.n

            @Sc
            def _(s, j=j, sl=sl, vvn=vvn):
                s.wait_ge(c_v.sem, vvn)
                inc(c_s, s.activation(out=dump[:, 0:C], in_=x1s[sl][:], func=AF.Square,
                                      accum_out=s_stat[:, 8 + j:9 + j]))

            @G
            def _(q, j=j, sl=sl, vvn=vvn):
                q.wait_ge(c_v.sem, vvn)
                inc(m_x1w, q.dma_start(out=x1_hbm[j * 128:(j + 1) * 128, :], in_=x1s[sl][:]), 16)

        maybe_stop(4)

        @V
        def _(v):
            v.wait_ge(c_s.sem, c_s.n)
            v.wait_ge(c_v.sem, c_v.n)
            inc(c_v, v.tensor_copy(out=s_stat[:, 0:8], in_=s_qcol[:, 48:56]))

        ln_math(EPS_LN)

        @V
        def _(v):
            v.wait_ge(m_row.sem, wscb_done)
            v.wait_ge(c_v.sem, c_v.n)
            mean = s_stat[:, 16:24]
            rs = s_stat[:, 24:32]
            t = s_qcol[:, 16:24]
            u = s_qcol[:, 24:32]
            vstep(v, v.tensor_tensor(out=t, in0=s_qcol[:, 0:8], in1=mean, op=ALU.subtract))
            vstep(v, v.tensor_tensor(out=u, in0=mean, in1=s_qcol[:, 8:16], op=ALU.subtract))
            vstep(v, v.tensor_tensor(out=t, in0=t, in1=u, op=ALU.max))
            vstep(v, v.tensor_tensor(out=t, in0=t, in1=rs, op=ALU.mult))
            vstep(v, v.tensor_scalar(out=t, in0=t, scalar1=EPS_Q, scalar2=INV_128,
                            op0=ALU.max, op1=ALU.mult))
            vstep(v, v.reciprocal(out=u, in_=t))
            vstep(v, v.tensor_tensor(out=s_qcol[:, 32:40], in0=rs, in1=u, op=ALU.mult))
            vstep(v, v.tensor_scalar(out=s_qcol[:, 40:48], in0=t, scalar1=s_wscb[:, 3:4],
                            scalar2=None, op0=ALU.mult))
            vstep(v, v.tensor_scalar(out=s_qcol[:, 48:56], in0=t, scalar1=s_wscb[:, 4:5],
                                     scalar2=None, op0=ALU.mult))
            vstep(v, v.reciprocal(out=s_qcol[:, 56:64], in_=s_qcol[:, 32:40]))
            vstep(v, v.tensor_scalar(out=s_qcol[:, 56:64], in0=s_qcol[:, 56:64], scalar1=-0.5,
                                     scalar2=None, op0=ALU.mult))
            inc(c_v, v.tensor_tensor(out=s_qcol[:, 56:64], in0=s_qcol[:, 56:64],
                                     in1=s_stat[:, 16:24], op=ALU.add))
        qcol_n = c_v.n

        join()
        a2ctx.close(); open_stacks.remove(a2ctx)
        actx.close(); open_stacks.remove(actx)

        bctx = ExitStack()
        open_stacks.append(bctx)
        be = bctx.enter_context
        qT = be(nc.sbuf_tensor("qT", [128, NCT, T], BF16))
        q_tok = [be(nc.sbuf_tensor(f"qtok{i}", [128, C], BF16)) for i in range(2)]
        tern_i = [be(nc.sbuf_tensor(f"terni{i}", [128, C], I32)) for i in range(2)]
        tcm = [be(nc.sbuf_tensor(f"tcm{i}", [128, C], F32)) for i in range(2)]
        ternbf = [be(nc.sbuf_tensor(f"ternbf{i}", [128, C], BF16)) for i in range(2)]
        tcb = [be(nc.sbuf_tensor(f"tcb{i}", [128, C], BF16)) for i in range(2)]
        wstr = [be(nc.sbuf_tensor(f"wstr{i}", [128, C], BF16)) for i in range(2)]
        csg_b = be(nc.sbuf_tensor("csg_b", [128, T], F32))
        gt = be(nc.sbuf_tensor("gt", [128, T], F32))
        gs = be(nc.sbuf_tensor("gs", [128, T], F32))
        itile = be(nc.sbuf_tensor("itile", [128, T], BF16))
        zsb = [be(nc.sbuf_tensor(f"zsb{i}", [128, T], BF16)) for i in range(2)]

        @Sy
        def _(q):
            q.wait_ge(c_v.sem, qcol_n)
            inc(m_row, q.dma_start(out=row_hbm[0:8, :].rearrange("a b -> b a"),
                                   in_=s_qcol[:, 40:48], allow_slow_non_contiguous=True), 16)
            q.wait_ge(m_row.sem, m_row.n)
            inc(m_row, q.dma_start(out=csg_b[:], in_=row_hbm.rearrange("a b -> (a b)")[0:T]
                                   .rearrange("(a b) -> a b", a=1).partition_broadcast(128)[:, 0, :]), 16)
        cs_b_n = m_row.n

        for j in range(NT):
            sl = j % 2

            @Sy
            def _(q, j=j, sl=sl):
                q.wait_ge(m_x1w.sem, m_x1w.n)
                q.wait_ge(c_v.sem, c_v.n)
                inc(m_ld[sl], q.dma_start(out=x1s[sl][:], in_=x1_hbm[j * 128:(j + 1) * 128, :]), 16)
            ldn = m_ld[sl].n

            mtr_prev = m_tr.n

            @V
            def _(v, j=j, sl=sl, ldn=ldn, mtr_prev=mtr_prev):
                v.wait_ge(m_ld[sl].sem, ldn)
                if j >= 2:
                    v.wait_ge(m_tr.sem, mtr_prev)
                vstep(v, v.tensor_scalar(out=xw[sl][:], in0=x1s[sl][:],
                                scalar1=s_qcol[:, 56 + j:57 + j],
                                scalar2=s_qcol[:, 32 + j:33 + j],
                                op0=ALU.subtract, op1=ALU.mult))
                vstep(v, v.tensor_scalar(out=xw[sl][:], in0=xw[sl][:], scalar1=127.5,
                                scalar2=-127.5, op0=ALU.min, op1=ALU.max))
                vstep(v, v.tensor_copy(out=tern_i[sl][:], in_=xw[sl][:]))
                vstep(v, v.tensor_tensor(out=tcm[sl][:], in0=tern_i[sl][:], in1=xw[sl][:],
                                         op=ALU.is_gt))
                vstep(v, v.tensor_tensor(out=q_tok[sl][:], in0=tern_i[sl][:],
                                         in1=tcm[sl][:], op=ALU.subtract))
                vstep(v, v.tensor_tensor(out=tcm[sl][:], in0=q_tok[sl][:], in1=xw[sl][:],
                                         op=ALU.is_gt))
                vstep(v, v.tensor_tensor(out=q_tok[sl][:], in0=q_tok[sl][:], in1=tcm[sl][:],
                                         op=ALU.subtract))
                vstep(v, v.tensor_tensor(out=tcm[sl][:], in0=xw[sl][:], in1=q_tok[sl][:],
                                         op=ALU.subtract))
                vstep(v, v.tensor_scalar(out=tcm[sl][:], in0=tcm[sl][:], scalar1=1.0,
                                         scalar2=None, op0=ALU.is_ge))
                inc(c_v, v.tensor_tensor(out=q_tok[sl][:], in0=q_tok[sl][:], in1=tcm[sl][:],
                                         op=ALU.add))
            vvn = c_v.n

            mtr_all = m_tr.n

            @Sc
            def _(s, j=j, sl=sl, vvn=vvn, mtr_all=mtr_all):
                s.wait_ge(c_v.sem, vvn)
                s.wait_ge(m_tr.sem, mtr_all)
                issue_transpose(s, m_tr, qT[:, :, j * 128:(j + 1) * 128],
                                q_tok[sl].ap().rearrange("p (k f) -> p k f", k=NCT))

            if KDBG:
                @G
                def _(q, j=j, sl=sl, vvn=vvn):
                    q.wait_ge(c_v.sem, vvn)
                    inc(m_dbg, q.dma_start(out=dbg_q[j * 128:(j + 1) * 128, :], in_=q_tok[sl][:]), 16)
        qT_n = m_tr.n
        maybe_stop(5)

        @Sc
        def _(s):
            s.preload_act_table(AF.Silu)

        tern_done = [0, 0]
        mm_done = [0, 0]
        ps_free = [0, 0]
        it_cons = 0
        zev = [0, 0]
        for g in range(NG):
            for wi, (wsrc, widx) in enumerate(((wg, 0), (wu, 1))):
                sl = wi

                @G
                def _(q, g=g, sl=sl, wsrc=wsrc, td=tern_done[sl]):
                    if td > 0:
                        q.wait_ge(c_v.sem, td)
                    inc(m_fw[sl], q.dma_start(out=wstr[sl][:],
                                              in_=wsrc[g * 128:(g + 1) * 128, :]), 16)
                ldn = m_fw[sl].n

                @V
                def _(v, sl=sl, ldn=ldn, widx=widx, md=mm_done[sl]):
                    v.wait_ge(m_fw[sl].sem, ldn)
                    v.wait_ge(m_row.sem, wscb_done)
                    if md > 0:
                        v.wait_ge(c_t.sem, md)
                    vstep(v, v.tensor_scalar(out=tcb[sl][:], in0=wstr[sl][:],
                                    scalar1=s_wscb[:, widx:widx + 1], scalar2=None,
                                    op0=ALU.is_ge))
                    vstep(v, v.tensor_scalar(out=wstr[sl][:], in0=wstr[sl][:],
                                    scalar1=s_wscb[:, 6 + widx:7 + widx], scalar2=None,
                                    op0=ALU.is_le))
                    inc(c_v, v.tensor_tensor(out=ternbf[sl][:], in0=tcb[sl][:], in1=wstr[sl][:],
                                             op=ALU.subtract))
                tern_done[sl] = c_v.n
                vvn = c_v.n

                pq = p_a if wi == 0 else p_b

                @Tn
                def _(t, sl=sl, vvn=vvn, pq=pq, g=g, psf=ps_free[wi]):
                    t.wait_ge(c_v.sem, vvn)
                    t.wait_ge(m_tr.sem, qT_n)
                    if psf > 0:
                        t.wait_ge(c_v.sem, psf)
                    last = None
                    for ct in range(NCT):
                        for th in range(2):
                            last = t.matmul(pq[:, th * 512:(th + 1) * 512],
                                            ternbf[sl][:, ct * 128:(ct + 1) * 128],
                                            qT[:, ct, th * 512:(th + 1) * 512],
                                            start=(ct == 0), stop=(ct == NCT - 1))
                    inc(c_t, last)
                mm_done[sl] = c_t.n

            tcn = c_t.n
            zsl = g % 2
            pz = p_z if g % 2 == 0 else p_z2

            @V
            def _(v, g=g, tcn=tcn):
                v.wait_ge(c_t.sem, tcn)
                v.wait_ge(c_v.sem, c_v.n)
                v.wait_ge(c_s.sem, c_s.n)
                inc(c_v, v.tensor_tensor(out=gt[:], in0=p_a[:], in1=csg_b[:], op=ALU.mult))
            ps_free[0] = c_v.n
            gt_n = c_v.n

            @Sc
            def _(s, g=g, gt_n=gt_n):
                s.wait_ge(c_v.sem, gt_n)
                inc(c_s, s.activation(out=gs[:], in_=gt[:], func=AF.Silu))
            ssn = c_s.n

            @V
            def _(v, g=g, ssn=ssn, itc=it_cons):
                v.wait_ge(c_s.sem, ssn)
                if itc > 0:
                    v.wait_ge(c_t.sem, itc)
                inc(c_v, v.tensor_tensor(out=itile[:], in0=gs[:], in1=p_b[:], op=ALU.mult))
            ps_free[1] = c_v.n
            ivn = c_v.n
            if KDBG and g == 0:
                @G
                def _(q, ivn=ivn):
                    q.wait_ge(c_v.sem, ivn)
                    inc(m_dbg, q.dma_start(out=dbg_g, in_=gt[:]), 16)
                    inc(m_dbg, q.dma_start(out=dbg_i, in_=itile[:]), 16)

            @Tn
            def _(t, g=g, ivn=ivn, pz=pz, zv=zev[g % 2]):
                t.wait_ge(c_v.sem, ivn)
                if zv > 0:
                    t.wait_ge(c_s.sem, zv)
                last = None
                for th in range(2):
                    last = t.matmul(pz[:, th * 512:(th + 1) * 512], s_h128[:],
                                    itile[:, th * 512:(th + 1) * 512],
                                    start=True, stop=True)
                inc(c_t, last)
            it_cons = c_t.n
            tzn = c_t.n

            @Sc
            def _(s, g=g, zsl=zsl, tzn=tzn, pz=pz):
                s.wait_ge(c_t.sem, tzn)
                if g >= 2:
                    s.wait_ge(m_zw.sem, m_zw.n)
                inc(c_s, s.activation(out=zsb[zsl][:], in_=pz[:], func=AF.Copy))
            zev[g % 2] = c_s.n
            szn = c_s.n

            @Sy
            def _(q, g=g, zsl=zsl, szn=szn):
                q.wait_ge(c_s.sem, szn)
                inc(m_zw, q.dma_start(
                    out=z_hbm.rearrange("(a b) t -> a b t", b=NG)[:, g, :],
                    in_=zsb[zsl][:]), 16)
        z_all = m_zw.n
        maybe_stop(6)

        join()
        bctx.close(); open_stacks.remove(bctx)

        cdctx = ExitStack()
        open_stacks.append(cdctx)
        ce = cdctx.enter_context
        s2_b = ce(nc.sbuf_tensor("s2_b", [128, T], F32))
        cs2d_b = ce(nc.sbuf_tensor("cs2d_b", [128, T], F32))
        cctx = ExitStack()
        open_stacks.append(cctx)
        cce = cctx.enter_context
        zpr = [cce(nc.sbuf_tensor(f"zpr{i}", [128, 2 * T], BF16)) for i in range(2)]
        hbf = [cce(nc.sbuf_tensor(f"hbf{i}", [128, 2 * T], BF16)) for i in range(2)]
        ysc = cce(nc.sbuf_tensor("ysc", [128, 2 * T], F32))
        yin = cce(nc.sbuf_tensor("yin", [128, 2 * T], I32))
        mctx = ExitStack()
        open_stacks.append(mctx)
        mxa = mctx.enter_context(nc.sbuf_tensor("mxa", [128, T], BF16))
        mxT = mctx.enter_context(nc.sbuf_tensor("mxT", [128, NT, 128], BF16))

        z2 = z_hbm

        zslot = [0, 0]
        pzAB = [None, None]
        hb_cons = [0, 0]
        for u in range(NG // 2):
            sl = u % 2

            @Sy
            def _(q, u=u, sl=sl, zs=zslot[sl]):
                q.wait_ge(m_zw.sem, z_all)
                if zs > 0:
                    q.wait_ge(c_t.sem, zs)
                inc(m_z[sl], q.dma_start(
                    out=zpr[sl].ap().rearrange("p (a t) -> p a t", a=2),
                    in_=z2[u * 256:(u + 1) * 256, :].rearrange("(a p) t -> p a t", p=128)), 16)
            ldn = m_z[sl].n

            @Tn
            def _(t, u=u, sl=sl, ldn=ldn, pva=pzAB[0], pvb=pzAB[1]):
                t.wait_ge(m_z[sl].sem, ldn)
                for pv in (pva, pvb):
                    if pv is not None:
                        t.wait_ge(pv[0].sem, pv[1])
                last = None
                for half, pz in ((0, p_z), (1, p_z2)):
                    for th in range(2):
                        o = half * 1024 + th * 512
                        last = t.matmul(pz[:, th * 512:(th + 1) * 512], s_h64b[:],
                                        zpr[sl][:, o:o + 512], start=True, stop=True)
                inc(c_t, last)
            zslot[sl] = c_t.n
            tcn = c_t.n

            @Sc
            def _(s, u=u, sl=sl, tcn=tcn, hv=hb_cons[sl]):
                s.wait_ge(c_t.sem, tcn)
                if hv > 0:
                    s.wait_ge(c_v.sem, hv)
                s.activation(out=hbf[sl][:, 0:T], in_=p_z[:], func=AF.Abs)
                inc(c_s, s.activation(out=hbf[sl][:, T:2 * T], in_=p_z2[:], func=AF.Abs))
            pzAB[0] = (c_s, c_s.n)
            pzAB[1] = (c_s, c_s.n)
            ssn = c_s.n

            @V
            def _(v, u=u, sl=sl, ssn=ssn):
                v.wait_ge(c_s.sem, ssn)
                v.wait_ge(c_v.sem, c_v.n)
                vstep(v, v.tensor_tensor(out=hbf[sl][:, 0:T], in0=hbf[sl][:, 0:T],
                                         in1=hbf[sl][:, T:2 * T], op=ALU.max))
                if u == 0:
                    inc(c_v, v.tensor_copy(out=mxa[:], in_=hbf[sl][:, 0:T]))
                else:
                    inc(c_v, v.tensor_tensor(out=mxa[:], in0=hbf[sl][:, 0:T], in1=mxa[:],
                                             op=ALU.max))
            hb_cons[sl] = c_v.n
        maxn = c_v.n

        @Sc
        def _(s):
            s.wait_ge(c_v.sem, maxn)
            issue_transpose(s, m_tr, mxT[:],
                            mxa.ap().rearrange("p (k f) -> p k f", k=NT))
        trn = m_tr.n

        @V
        def _(v):
            v.wait_ge(m_tr.sem, trn)
            vstep(v, v.reduce_max(out=s_qcol[:, 0:8], in_=mxT[:], axis=AX.X))
            vstep(v, v.tensor_scalar(out=s_qcol[:, 8:16], in0=s_qcol[:, 0:8], scalar1=EPS_Q,
                            scalar2=INV_128, op0=ALU.max, op1=ALU.mult))
            vstep(v, v.reciprocal(out=s_qcol[:, 16:24], in_=s_qcol[:, 8:16]))
            vstep(v, v.tensor_scalar(out=s_qcol[:, 24:32], in0=s_qcol[:, 8:16],
                                     scalar1=s_wscb[:, 5:6], scalar2=None, op0=ALU.mult))
            inc(c_v, v.tensor_tensor(out=s_qcol[:, 24:32], in0=s_qcol[:, 24:32],
                                     in1=s_qcol[:, 48:56], op=ALU.mult))
        s2n = c_v.n

        @Sy
        def _(q):
            q.wait_ge(c_v.sem, s2n)
            inc(m_row, q.dma_start(out=row_hbm[0:8, :].rearrange("a b -> b a"),
                                   in_=s_qcol[:, 16:24], allow_slow_non_contiguous=True), 16)
            q.wait_ge(m_row.sem, m_row.n)
            inc(m_row, q.dma_start(out=s2_b[:], in_=row_hbm.rearrange("a b -> (a b)")[0:T]
                                   .rearrange("(a b) -> a b", a=1).partition_broadcast(128)[:, 0, :]), 16)
            q.wait_ge(m_row.sem, m_row.n)
            inc(m_row, q.dma_start(out=row_hbm[0:8, :].rearrange("a b -> b a"),
                                   in_=s_qcol[:, 24:32], allow_slow_non_contiguous=True), 16)
            q.wait_ge(m_row.sem, m_row.n)
            inc(m_row, q.dma_start(out=cs2d_b[:], in_=row_hbm.rearrange("a b -> (a b)")[0:T]
                                   .rearrange("(a b) -> a b", a=1).partition_broadcast(128)[:, 0, :]), 16)
        s2b_n = m_row.n
        mctx.close(); open_stacks.remove(mctx)
        q2s = [cce(nc.sbuf_tensor(f"q2s{i}", [128, 2 * T], BF16)) for i in range(2)]
        hg = cce(nc.sbuf_tensor("hg", [128, 2 * T], F32))
        maybe_stop(7)

        for u in range(NG // 2):
            sl = u % 2

            @Sy
            def _(q, u=u, sl=sl, zs=zslot[sl]):
                q.wait_ge(m_row.sem, s2b_n)
                if zs > 0:
                    q.wait_ge(c_t.sem, zs)
                inc(m_z[sl], q.dma_start(
                    out=zpr[sl].ap().rearrange("p (a t) -> p a t", a=2),
                    in_=z2[u * 256:(u + 1) * 256, :].rearrange("(a p) t -> p a t", p=128)), 16)
            ldn = m_z[sl].n

            @Tn
            def _(t, u=u, sl=sl, ldn=ldn, pva=pzAB[0], pvb=pzAB[1]):
                t.wait_ge(m_z[sl].sem, ldn)
                for pv in (pva, pvb):
                    if pv is not None:
                        t.wait_ge(pv[0].sem, pv[1])
                last = None
                for half, pz in ((0, p_z), (1, p_z2)):
                    for th in range(2):
                        o = half * 1024 + th * 512
                        last = t.matmul(pz[:, th * 512:(th + 1) * 512], s_h64b[:],
                                        zpr[sl][:, o:o + 512], start=True, stop=True)
                inc(c_t, last)
            zslot[sl] = c_t.n
            tcn = c_t.n

            @V
            def _(v, u=u, sl=sl, tcn=tcn, q2w=m_q2w.n):
                v.wait_ge(c_t.sem, tcn)
                v.wait_ge(c_v.sem, c_v.n)
                if u >= 2:
                    v.wait_ge(m_q2w.sem, q2w)
                vstep(v, v.tensor_tensor(out=ysc[:, 0:T], in0=p_z[:], in1=s2_b[:],
                                         op=ALU.mult))
                vstep(v, v.tensor_tensor(out=ysc[:, T:2 * T], in0=p_z2[:], in1=s2_b[:],
                                         op=ALU.mult))
                vstep(v, v.tensor_scalar(out=ysc[:], in0=ysc[:], scalar1=0.5, scalar2=127.5,
                                         op0=ALU.add, op1=ALU.min))
                vstep(v, v.tensor_scalar(out=ysc[:], in0=ysc[:], scalar1=-127.5, scalar2=None,
                                         op0=ALU.max))
                vstep(v, v.tensor_copy(out=yin[:], in_=ysc[:]))
                vstep(v, v.tensor_tensor(out=hg[:], in0=yin[:], in1=ysc[:], op=ALU.is_gt))
                vstep(v, v.tensor_tensor(out=q2s[sl][:], in0=yin[:], in1=hg[:],
                                         op=ALU.subtract))
                vstep(v, v.tensor_tensor(out=hg[:], in0=q2s[sl][:], in1=ysc[:], op=ALU.is_gt))
                vstep(v, v.tensor_tensor(out=q2s[sl][:], in0=q2s[sl][:], in1=hg[:],
                                         op=ALU.subtract))
                vstep(v, v.tensor_tensor(out=hg[:], in0=ysc[:], in1=q2s[sl][:],
                                         op=ALU.subtract))
                vstep(v, v.tensor_scalar(out=hg[:], in0=hg[:], scalar1=1.0, scalar2=None,
                                         op0=ALU.is_ge))
                inc(c_v, v.tensor_tensor(out=q2s[sl][:], in0=q2s[sl][:], in1=hg[:],
                                         op=ALU.add))
            pzAB[0] = (c_v, c_v.n - 11)
            pzAB[1] = (c_v, c_v.n - 10)
            qvn = c_v.n

            @G
            def _(q, u=u, sl=sl, qvn=qvn):
                q.wait_ge(c_v.sem, qvn)
                qv = q2_hbm.rearrange("(j1 j2) t -> j1 j2 t", j2=128)
                for half in range(2):
                    jp = 2 * u + half
                    inc(m_q2w, q.dma_start(out=qv[:, 2 * jp, :],
                                           in_=q2s[sl][0:64, half * T:(half + 1) * T]), 16)
                    inc(m_q2w, q.dma_start(out=qv[:, 2 * jp + 1, :],
                                           in_=q2s[sl][64:128, half * T:(half + 1) * T]), 16)
        q2_all = m_q2w.n
        maybe_stop(8)

        join()
        cctx.close(); open_stacks.remove(cctx)
        q2r = ce(nc.sbuf_tensor("q2r", [128, NG, T], BF16))

        @Sy
        def _(q):
            q.wait_ge(m_q2w.sem, q2_all)
            for k in range(NG):
                inc(m_q2r, q.dma_start(out=q2r[:, k, :], in_=q2_hbm[k * 128:(k + 1) * 128, :]), 16)
        q2r_n = m_q2r.n

        CHK = 1024
        NKC = 8
        wstrD = [ce(nc.sbuf_tensor(f"wstrD{i}", [128, CHK], BF16)) for i in range(2)]
        t1D = [ce(nc.sbuf_tensor(f"t1D{i}", [128, CHK], BF16)) for i in range(2)]
        ternbD = [ce(nc.sbuf_tensor(f"ternbD{i}", [128, CHK], BF16)) for i in range(2)]
        dnbf = [ce(nc.sbuf_tensor(f"dnbf{i}", [128, T], BF16)) for i in range(2)]

        ternD_done = [0, 0]
        mmD_done = [0, 0]
        dps_free = [0, 0]
        for m in range(NCT):
            pq = p_a if m % 2 == 0 else p_b
            for qx in range(8):
                i = 8 * m + qx
                sl = i % 2

                @G
                def _(q, m=m, qx=qx, sl=sl, td=ternD_done[sl]):
                    if td > 0:
                        q.wait_ge(c_v.sem, td)
                    inc(m_fw[sl], q.dma_start(
                        out=wstrD[sl][:],
                        in_=wd[m * 128:(m + 1) * 128, qx * 1024:(qx + 1) * 1024]), 16)
                ldn = m_fw[sl].n

                @V
                def _(v, sl=sl, ldn=ldn, md=mmD_done[sl]):
                    v.wait_ge(m_fw[sl].sem, ldn)
                    if md > 0:
                        v.wait_ge(c_t.sem, md)
                    vstep(v, v.tensor_scalar(out=t1D[sl][:], in0=wstrD[sl][:],
                                    scalar1=s_wscb[:, 2:3], scalar2=None, op0=ALU.is_ge))
                    vstep(v, v.tensor_scalar(out=wstrD[sl][:], in0=wstrD[sl][:],
                                    scalar1=s_wscb[:, 8:9], scalar2=None, op0=ALU.is_le))
                    inc(c_v, v.tensor_tensor(out=ternbD[sl][:], in0=t1D[sl][:], in1=wstrD[sl][:],
                                             op=ALU.subtract))
                ternD_done[sl] = c_v.n
                vvn = c_v.n

                @Tn
                def _(t, m=m, qx=qx, sl=sl, vvn=vvn, pq=pq, psf=dps_free[m % 2]):
                    t.wait_ge(c_v.sem, vvn)
                    t.wait_ge(m_q2r.sem, q2r_n)
                    if qx == 0 and psf > 0:
                        t.wait_ge(c_v.sem, psf)
                    last = None
                    for ct in range(NKC):
                        k = qx * NKC + ct
                        for th in range(2):
                            last = t.matmul(pq[:, th * 512:(th + 1) * 512],
                                            ternbD[sl][:, ct * 128:(ct + 1) * 128],
                                            q2r[:, k, th * 512:(th + 1) * 512],
                                            start=(k == 0), stop=(k == NG - 1))
                    inc(c_t, last)
                mmD_done[sl] = c_t.n

            tcn = c_t.n
            dsl = m % 2

            @V
            def _(v, m=m, dsl=dsl, tcn=tcn, pq=pq):
                v.wait_ge(c_t.sem, tcn)
                v.wait_ge(c_v.sem, c_v.n)
                if m >= 2:
                    v.wait_ge(m_dnw.sem, m_dnw.n)
                inc(c_v, v.tensor_tensor(out=dnbf[dsl][:], in0=pq[:], in1=cs2d_b[:],
                                         op=ALU.mult))
            dps_free[m % 2] = c_v.n
            dvn = c_v.n

            @G
            def _(q, m=m, dsl=dsl, dvn=dvn):
                q.wait_ge(c_v.sem, dvn)
                inc(m_dnw, q.dma_start(out=dn_hbm[m * 128:(m + 1) * 128, :], in_=dnbf[dsl][:]), 16)
        dn_all = m_dnw.n
        maybe_stop(9)

        join()
        cdctx.close(); open_stacks.remove(cdctx)

        ectx = ExitStack()
        open_stacks.append(ectx)
        ee = ectx.enter_context
        dntok = ee(nc.sbuf_tensor("dntok", [128, NT, C], BF16))
        dumpE = ee(nc.sbuf_tensor("dumpE", [128, C], F32))
        outb = [ee(nc.sbuf_tensor(f"outb{i}", [128, C], F32)) for i in range(2)]

        @Sc
        def _(s):
            s.preload_act_table(AF.Sqrt)
            s.wait_ge(m_dnw.sem, dn_all)
            for k in range(NCT):
                issue_transpose(s, m_tr, dntok[:, :, k * 128:(k + 1) * 128],
                                dn_hbm[k * 128:(k + 1) * 128, :]
                                .rearrange("p (j f) -> p j f", j=NT))
        dnt_n = m_tr.n

        for j in range(NT):
            @Sc
            def _(s, j=j):
                s.wait_ge(m_tr.sem, dnt_n)
                inc(c_s, s.activation(out=dumpE[:], in_=dntok[:, j, :], func=AF.Square,
                                      accum_out=s_stat[:, 8 + j:9 + j]))

            @V
            def _(v, j=j):
                v.wait_ge(m_tr.sem, dnt_n)
                inc(c_v, v.reduce_sum(out=s_stat[:, j:j + 1], in_=dntok[:, j, :], axis=AX.X))

        ln_math(EPS_LN)
        lnf_rs = c_v.n

        for j in range(NT):
            sl = j % 2

            @Sy
            def _(q, j=j, sl=sl):
                q.wait_ge(c_v.sem, c_v.n)
                if j >= 2:
                    q.wait_ge(m_out.sem, m_out.n)
                inc(m_ld[sl], q.dma_start(out=x1s[sl][:], in_=x1_hbm[j * 128:(j + 1) * 128, :]), 16)
            ldn = m_ld[sl].n

            @V
            def _(v, j=j, sl=sl, ldn=ldn):
                v.wait_ge(m_ld[sl].sem, ldn)
                v.wait_ge(c_v.sem, lnf_rs)
                vstep(v, v.tensor_scalar(out=outb[sl][:], in0=dntok[:, j, :],
                                scalar1=s_stat[:, 16 + j:17 + j],
                                scalar2=s_stat[:, 24 + j:25 + j],
                                op0=ALU.subtract, op1=ALU.mult))
                inc(c_v, v.tensor_tensor(out=outb[sl][:], in0=outb[sl][:], in1=x1s[sl][:],
                                         op=ALU.add))
            ovn = c_v.n

            @Sy
            def _(q, j=j, sl=sl, ovn=ovn):
                q.wait_ge(c_v.sem, ovn)
                inc(m_out, q.dma_start(out=o_out[j * 128:(j + 1) * 128, :], in_=outb[sl][:]), 16)

        @Sy
        def _(q):
            q.wait_ge(m_out.sem, m_out.n)

        ectx.close(); open_stacks.remove(ectx)
    except _Stop:
        for st in reversed(open_stacks):
            st.close()
    ctx.close()
    return nc


def _hadamard(n):
    h = np.array([[1.0]], dtype=np.float64)
    while h.shape[0] < n:
        h = np.block([[h, h], [h, -h]])
    return h


def kernel(L0_attn_x, L0_attn_time_first, L0_attn_time_mix_k, L0_attn_time_mix_v,
           L0_attn_time_mix_r, L0_attn_xx, L0_attn_aa, L0_attn_bb, L0_attn_pp,
           L0_attn_w_key, L0_attn_w_value, L0_attn_w_rec, L0_attn_w_out,
           L0_ffn_w_gate, L0_ffn_w_up, L0_ffn_w_down):
    nc = build_nc()

    def col_form(v):
        return np.ascontiguousarray(np.asarray(v, dtype=np.float32)
                                    .reshape(NCT, 128).T)

    h64 = _hadamard(64)
    h128 = np.ascontiguousarray(_hadamard(128).astype(np.float32))
    c32 = np.float32(1.0 / np.sqrt(np.float64(II)))
    h64b = np.zeros((128, 128), dtype=np.float32)
    h64b[0:64, 0:64] = (h64 * np.float64(c32)).astype(np.float32)
    h64b[64:128, 64:128] = h64b[0:64, 0:64]

    def blk_oc(w):
        o, ci = w.shape
        return np.ascontiguousarray(
            np.asarray(w, dtype=np.float32).reshape(o // 128, 128, ci // 128, 128)
            .transpose(0, 3, 2, 1).reshape(o, ci))

    shared = {
        "wk": blk_oc(L0_attn_w_key),
        "wv": blk_oc(L0_attn_w_value),
        "wr": blk_oc(L0_attn_w_rec),
        "wo": blk_oc(L0_attn_w_out),
        "wg": blk_oc(L0_ffn_w_gate),
        "wu": blk_oc(L0_ffn_w_up),
        "wd": blk_oc(L0_ffn_w_down),
        "tmk": col_form(L0_attn_time_mix_k),
        "tmv": col_form(L0_attn_time_mix_v),
        "tmr": col_form(L0_attn_time_mix_r),
        "tfc": col_form(L0_attn_time_first),
        "xxc": col_form(L0_attn_xx),
        "aac": col_form(L0_attn_aa),
        "bbc": col_form(L0_attn_bb),
        "ppc": col_form(L0_attn_pp),
        "h128": h128,
        "h64b": h64b,
    }
    xfull = np.ascontiguousarray(np.asarray(L0_attn_x, dtype=np.float32)
                                 .reshape(8192, C))
    in_maps = []
    for c in range(NCORES_G):
        m = dict(shared)
        m["x"] = np.ascontiguousarray(xfull[c * T:(c + 1) * T, :])
        m["flag"] = np.array([[1.0 if c == 0 else 0.0]], dtype=np.float32)
        in_maps.append(m)

    res = run_bass_kernel_spmd(nc, in_maps, core_ids=list(range(NCORES_G)))
    out = np.concatenate([r["out"] for r in res.results], axis=0)
    return out.reshape(1, 8192, C).astype(np.float32)



# version 18
# speedup vs baseline: 1.1019x; 1.1019x over previous
import os
import numpy as np
from contextlib import ExitStack

import concourse.bass as bass
import concourse.mybir as mybir
from concourse.bass_utils import run_bass_kernel_spmd

F32 = mybir.dt.float32
BF16 = mybir.dt.bfloat16
I32 = mybir.dt.int32
AF = mybir.ActivationFunctionType
ALU = mybir.AluOpType
AX = mybir.AxisListType

NCORES_G = 8
T = 1024
C = 2048
II = 8192
NT = 8
NCT = 16
NG = 64
EPS_LN = 1e-5
EPS_Q = 1e-5
INV_C = 1.0 / 2048.0
INV_128 = 1.0 / 128.0
INV_NW = 1.0 / 16777216.0


class Ctr:
    def __init__(self, sem):
        self.sem = sem
        self.n = 0


def inc(ctr, instr, k=1):
    ctr.n += k
    instr.then_inc(ctr.sem, k)


class _Stop(Exception):
    pass


def build_nc():
    nc = bass.Bass("TRN2", target_bir_lowering=False, detect_race_conditions=False)
    PHASE_LIMIT = int(os.environ.get("KPHASE", "99"))

    x_in = nc.dram_tensor("x", [T, C], F32, kind="ExternalInput").ap()
    w_at = {nm: nc.dram_tensor(nm, [C, C], F32, kind="ExternalInput").ap()
            for nm in ("wk", "wv", "wr", "wo")}
    wg = nc.dram_tensor("wg", [II, C], F32, kind="ExternalInput").ap()
    wu = nc.dram_tensor("wu", [II, C], F32, kind="ExternalInput").ap()
    wd = nc.dram_tensor("wd", [C, II], F32, kind="ExternalInput").ap()
    cv_in = {nm: nc.dram_tensor(nm, [128, NCT], F32, kind="ExternalInput").ap()
             for nm in ("tmk", "tmv", "tmr", "tfc", "xxc", "aac", "bbc", "ppc")}
    flag = nc.dram_tensor("flag", [1, 1], F32, kind="ExternalInput").ap()
    h128 = nc.dram_tensor("h128", [128, 128], F32, kind="ExternalInput").ap()
    h64b = nc.dram_tensor("h64b", [128, 128], F32, kind="ExternalInput").ap()

    o_out = nc.dram_tensor("out", [T, C], F32, kind="ExternalOutput").ap()
    _scr_kind = "ExternalOutput" if os.environ.get("KDUMP") else "Internal"
    z_hbm = nc.dram_tensor("z_scr", [128 * NG, T], BF16, kind=_scr_kind).ap()
    q2_hbm = nc.dram_tensor("q2_scr", [II, T], BF16, kind=_scr_kind).ap()
    x1_hbm = nc.dram_tensor("x1_scr", [T, C], F32, kind=_scr_kind).ap()
    dn_hbm = nc.dram_tensor("dn_scr", [C, T], BF16, kind=_scr_kind).ap()
    row_hbm = nc.dram_tensor("row_scr", [16, 128], F32, kind="Internal").ap()
    r3_hbm = nc.dram_tensor("r3_scr", [3, 128], F32, kind="Internal").ap()
    sc_hbm = nc.dram_tensor("sc_scr", [1, 12], F32, kind="Internal").ap()
    KDBG = bool(os.environ.get("KDBG"))
    if KDBG:
        dbg_q = nc.dram_tensor("dbg_q", [T, C], BF16, kind="Internal").ap()
        dbg_ao = nc.dram_tensor("dbg_ao", [128, NT * C], BF16, kind="ExternalOutput").ap()
        dbg_st = nc.dram_tensor("dbg_st", [128, 80], F32, kind="ExternalOutput").ap()
        dbg_vb = nc.dram_tensor("dbg_vb", [128, NCT * T], BF16, kind="ExternalOutput").ap()
        dbg_ln1 = nc.dram_tensor("dbg_ln1", [128, NCT * T], BF16, kind="ExternalOutput").ap()
        dbg_st1 = nc.dram_tensor("dbg_st1", [128, 80], F32, kind="ExternalOutput").ap()
        dbg_g = nc.dram_tensor("dbg_g", [128, T], F32, kind="Internal").ap()
        dbg_i = nc.dram_tensor("dbg_i", [128, T], F32, kind="Internal").ap()

    ctx = ExitStack()
    en = ctx.enter_context
    open_stacks = []

    def sbuf(name, shape, dt=F32):
        return en(nc.sbuf_tensor(name, shape, dt))

    s_cv = {nm: sbuf("s_" + nm, [128, NCT]) for nm in cv_in}
    s_off = sbuf("s_off", [128, 3 * NCT])
    s_flag = sbuf("s_flag", [128, 1])
    s_h128 = sbuf("s_h128", [128, 128], BF16)
    s_h64b = sbuf("s_h64b", [128, 128], BF16)
    s_stat = sbuf("s_stat", [128, 80])
    s_sclp = sbuf("s_sclp", [128, 192])
    s_wrow = sbuf("s_wrow", [1, 384])
    s_wsc = sbuf("s_wsc", [1, 12])
    s_wscb = sbuf("s_wscb", [128, 12])
    s_qcol = sbuf("s_qcol", [128, 64])
    s_wkv = sbuf("s_wkv", [128, 12 * NCT])
    s_xk0 = sbuf("s_xk0", [128, NCT], BF16)
    s_v0 = sbuf("s_v0", [128, NCT])
    s_r0 = sbuf("s_r0", [128, NCT])

    xw = [sbuf(f"xw{i}", [128, C]) for i in range(2)]
    x1s = [sbuf(f"x1s{i}", [128, C]) for i in range(2)]

    p_a = en(nc.psum_tensor("p_a", [128, 1024], F32))
    p_b = en(nc.psum_tensor("p_b", [128, 1024], F32))
    p_z = en(nc.psum_tensor("p_z", [128, 1024], F32))
    p_z2 = en(nc.psum_tensor("p_z2", [128, 1024], F32))

    Sm = lambda nm: Ctr(en(nc.semaphore(nm)))
    m_cst = Sm("m_cst")
    m_ld = [Sm(f"m_ld{i}") for i in range(2)]
    m_w = [Sm(f"m_w{i}") for i in range(2)]
    m_scl = [Sm(f"m_scl{i}") for i in range(2)]
    m_fw = [Sm(f"m_fw{i}") for i in range(2)]
    m_z = [Sm(f"m_z{i}") for i in range(2)]
    m_zw = Sm("m_zw")
    m_q2w = Sm("m_q2w")
    m_q2r = Sm("m_q2r")
    m_x1w = Sm("m_x1w")
    m_dnw = Sm("m_dnw")
    m_row = Sm("m_row")
    m_tr = Sm("m_tr")
    m_out = Sm("m_out")
    m_dbg = Sm("m_dbg")
    c_t = Sm("c_t")
    c_v = Sm("c_v")
    c_s = Sm("c_s")
    c_g = Sm("c_g")
    scl_done = [Sm(f"scl_done{i}") for i in range(2)]

    block = en(nc.Block())
    Sy = block.sync
    V = block.vector
    Sc = block.scalar
    Tn = block.tensor
    G = block.gpsimd

    def vstep(v, instr):
        inc(c_v, instr)
        v.wait_ge(c_v.sem, c_v.n)

    last_tr = [None]

    def issue_transpose(s, ctr, out, in_):
        if last_tr[0] is not None:
            psem, pcnt = last_tr[0]
            s.wait_ge(psem, pcnt)
        inc(ctr, s.dma_start(out=out, in_=in_, transpose=True), 16)
        last_tr[0] = (ctr.sem, ctr.n)

    def join():
        pass

    def maybe_stop(pid):
        if PHASE_LIMIT <= pid:
            @Sy
            def _(q):
                q.wait_ge(c_v.sem, c_v.n)
                q.wait_ge(c_s.sem, c_s.n)
                q.wait_ge(c_t.sem, c_t.n)
                for j in range(NT):
                    inc(m_out, q.dma_start(out=o_out[j * 128:(j + 1) * 128, :],
                                           in_=x1s[j % 2][:]), 16)
                q.wait_ge(m_out.sem, m_out.n)
            raise _Stop()

    try:
        def ln_math(eps):
            centry_v, centry_s = c_v.n, c_s.n

            @V
            def _(v):
                mean = s_stat[:, 16:24]
                t0 = s_stat[:, 32:40]
                t1 = s_stat[:, 40:48]
                v.wait_ge(c_v.sem, centry_v)
                v.wait_ge(c_s.sem, centry_s)
                vstep(v, v.tensor_scalar(out=mean, in0=s_stat[:, 0:8], scalar1=INV_C,
                                scalar2=None, op0=ALU.mult))
                vstep(v, v.tensor_scalar(out=t0, in0=s_stat[:, 8:16], scalar1=INV_C,
                                scalar2=None, op0=ALU.mult))
                vstep(v, v.tensor_tensor(out=t1, in0=mean, in1=mean, op=ALU.mult))
                vstep(v, v.tensor_tensor(out=t0, in0=t0, in1=t1, op=ALU.subtract))
                vstep(v, v.tensor_scalar(out=t0, in0=t0, scalar1=eps, scalar2=None, op0=ALU.add))
                inc(c_v, v.reciprocal(out=t1, in_=t0))
            vn = c_v.n

            @Sc
            def _(s):
                s.wait_ge(c_v.sem, vn)
                inc(c_s, s.activation(out=s_stat[:, 48:56], in_=s_stat[:, 40:48],
                                      func=AF.Sqrt))
            sn = c_s.n

            @V
            def _(v):
                rs = s_stat[:, 24:32]
                t0 = s_stat[:, 32:40]
                t1 = s_stat[:, 40:48]
                rs0 = s_stat[:, 48:56]
                v.wait_ge(c_s.sem, sn)
                vstep(v, v.tensor_tensor(out=t1, in0=rs0, in1=rs0, op=ALU.mult))
                vstep(v, v.tensor_tensor(out=t1, in0=t1, in1=t0, op=ALU.mult))
                vstep(v, v.tensor_scalar(out=t1, in0=t1, scalar1=-0.5, scalar2=1.5,
                                op0=ALU.mult, op1=ALU.add))
                inc(c_v, v.tensor_tensor(out=rs, in0=rs0, in1=t1, op=ALU.mult))

        @Sy
        def _(q):
            for nm in cv_in:
                inc(m_cst, q.dma_start(out=s_cv[nm][:], in_=cv_in[nm]), 16)
            inc(m_cst, q.dma_start(out=s_flag[:],
                                   in_=flag.partition_broadcast(128)[:, 0, :]), 16)

        @G
        def _(q):
            inc(m_cst, q.dma_start(out=s_h128[:], in_=h128), 16)
            inc(m_cst, q.dma_start(out=s_h64b[:], in_=h64b), 16)
        cst_n = m_cst.n

        @Sc
        def _(s):
            s.preload_act_table(AF.Sqrt)

        @V
        def _(v):
            v.wait_ge(m_cst.sem, cst_n)
            v.memset(s_wsc[:], 0.0)
            v.memset(s_stat[:], 0.0)
            v.memset(s_qcol[:], 0.0)
            v.memset(s_wkv[:], 0.0)
            for idx, nm in enumerate(("tmk", "tmv", "tmr")):
                o = s_off[:, idx * NCT:(idx + 1) * NCT]
                vstep(v, v.tensor_scalar(out=o, in0=s_cv[nm][:], scalar1=-1.0, scalar2=1.0,
                                op0=ALU.mult, op1=ALU.add))
                inc(c_v, v.tensor_tensor(out=o, in0=o, in1=s_cv["xxc"][:], op=ALU.mult))
        off_done = c_v.n

        SCT = 128 * 2048
        wg_f = wg.rearrange("a b -> (a b)")
        wu_f = wu.rearrange("a b -> (a b)")
        wd_f = wd.rearrange("a b -> (a b)")
        scale_tiles = (
            [(wg_f[i * SCT:(i + 1) * SCT].rearrange("(p f) -> p f", p=128), "v", 0, i)
             for i in range(64)]
            + [(wu_f[i * SCT:(i + 1) * SCT].rearrange("(p f) -> p f", p=128), "s", 1, i)
               for i in range(64)]
            + [(wd_f[i * SCT:(i + 1) * SCT].rearrange("(p f) -> p f", p=128),
                "v" if i % 2 == 0 else "s", 2, i) for i in range(64)]
        )

        actx = ExitStack()
        open_stacks.append(actx)
        ae = actx.enter_context
        dump = ae(nc.sbuf_tensor("dumpA", [128, 2048], F32))
        scl = [ae(nc.sbuf_tensor(f"scl{i}", [128, 2048], F32)) for i in range(2)]
        wstrA = [ae(nc.sbuf_tensor(f"wstrA{i}", [128, C], BF16)) for i in range(2)]
        ln1_tok = [ae(nc.sbuf_tensor(f"ln1tok{i}", [128, C], BF16)) for i in range(2)]
        vbf = ae(nc.sbuf_tensor("vbf", [128, NCT, T], BF16))
        rt = [ae(nc.sbuf_tensor(f"rt{i}", [128, T], BF16)) for i in range(2)]
        s_k0r = ae(nc.sbuf_tensor("s_k0r", [1, C], F32))

        scl_cons = [0, 0]
        scale_iter = iter(enumerate(scale_tiles))

        def emit_scale(nmax):
            for _ in range(nmax):
                try:
                    i, (src, eng, col, sub) = next(scale_iter)
                except StopIteration:
                    return
                sl = i % 2

                @Sy
                def _(q, i=i, sl=sl, src=src):
                    if i >= 2:
                        q.wait_ge(scl_done[sl].sem, scl_cons[sl])
                    inc(m_scl[sl], q.dma_start(out=scl[sl][:], in_=src), 16)
                ldn = m_scl[sl].n

                if eng == "v":
                    @V
                    def _(v, i=i, sl=sl, ldn=ldn):
                        v.wait_ge(m_scl[sl].sem, ldn)
                        inc(scl_done[sl], v.reduce_sum(out=s_sclp[:, i:i + 1], in_=scl[sl][:],
                                                       axis=AX.X, apply_absolute_value=True))
                else:
                    @Sc
                    def _(s, i=i, sl=sl, ldn=ldn):
                        s.wait_ge(m_scl[sl].sem, ldn)
                        inc(scl_done[sl], s.activation(out=dump[:], in_=scl[sl][:], func=AF.Abs,
                                                       accum_out=s_sclp[:, i:i + 1]))
                scl_cons[sl] += 1


        for j in range(NT):
            sl = j % 2
            emit_scale(2)

            @Sy
            def _(q, j=j, sl=sl):
                if j >= 2:
                    q.wait_ge(c_s.sem, c_s.n)
                    q.wait_ge(c_v.sem, c_v.n)
                inc(m_ld[sl], q.dma_start(out=xw[sl][:], in_=x_in[j * 128:(j + 1) * 128, :]), 16)
            ldn = m_ld[sl].n

            @Sc
            def _(s, j=j, sl=sl, ldn=ldn):
                s.wait_ge(m_ld[sl].sem, ldn)
                inc(c_s, s.activation(out=dump[:, 0:C], in_=xw[sl][:], func=AF.Square,
                                      accum_out=s_stat[:, 8 + j:9 + j]))

            @V
            def _(v, j=j, sl=sl, ldn=ldn):
                v.wait_ge(m_ld[sl].sem, ldn)
                inc(c_v, v.reduce_sum(out=s_stat[:, j:j + 1], in_=xw[sl][:], axis=AX.X))

        ln_math(EPS_LN)
        ln1_rs = c_v.n

        lctx = ExitStack()
        open_stacks.append(lctx)
        ln1T = lctx.enter_context(nc.sbuf_tensor("ln1T", [128, NCT, T], BF16))
        xT = lctx.enter_context(nc.sbuf_tensor("xT", [128, NCT, T], BF16))

        for j in range(NT):
            sl = j % 2

            @Sy
            def _(q, j=j, sl=sl):
                q.wait_ge(c_v.sem, c_v.n)
                inc(m_ld[sl], q.dma_start(out=xw[sl][:], in_=x_in[j * 128:(j + 1) * 128, :]), 16)
            ldn = m_ld[sl].n
            mtr_all = m_tr.n

            @V
            def _(v, j=j, sl=sl, ldn=ldn, mtr_all=mtr_all):
                v.wait_ge(m_ld[sl].sem, ldn)
                if j >= 2:
                    v.wait_ge(m_tr.sem, mtr_all)
                inc(c_v, v.tensor_scalar(
                    out=ln1_tok[sl][:], in0=xw[sl][:],
                    scalar1=s_stat[:, 16 + j:17 + j], scalar2=s_stat[:, 24 + j:25 + j],
                    op0=ALU.subtract, op1=ALU.mult))
            nn_ = c_v.n

            @Sc
            def _(s, j=j, sl=sl, nn_=nn_):
                s.wait_ge(c_v.sem, nn_)
                issue_transpose(s, m_tr, ln1T[:, :, j * 128:(j + 1) * 128],
                                ln1_tok[sl].ap().rearrange("p (k f) -> p k f", k=NCT))
        ln1T_done = m_tr.n
        if KDBG:
            @G
            def _(q):
                q.wait_ge(m_tr.sem, ln1T_done)
                inc(m_dbg, q.dma_start(out=dbg_ln1, in_=ln1T.ap().rearrange("p a b -> p (a b)")), 16)
                inc(m_dbg, q.dma_start(out=dbg_st1, in_=s_stat[:]), 16)
        maybe_stop(1)

        def scale_finalize():
            @V
            def _(v):
                v.wait_ge(scl_done[0].sem, scl_cons[0])
                v.wait_ge(scl_done[1].sem, scl_cons[1])
                v.reduce_sum(out=s_stat[:, 60:61], in_=s_sclp[:, 0:64], axis=AX.X)
                v.reduce_sum(out=s_stat[:, 61:62], in_=s_sclp[:, 64:128], axis=AX.X)
                vstep(v, v.reduce_sum(out=s_stat[:, 62:63], in_=s_sclp[:, 128:192], axis=AX.X))
                inc(c_v, v.nop())
            vn = c_v.n

            @Sy
            def _(q):
                q.wait_ge(c_v.sem, vn)
                inc(m_row, q.dma_start(out=r3_hbm.rearrange("w p -> p w"), in_=s_stat[:, 60:63], allow_slow_non_contiguous=True), 16)
                q.wait_ge(m_row.sem, m_row.n)
                inc(m_row, q.dma_start(out=s_wrow[:], in_=r3_hbm.rearrange("w p -> (w p)").rearrange("(a b) -> a b", a=1)), 16)
            rown = m_row.n

            @V
            def _(v):
                v.wait_ge(m_row.sem, rown)
                for w in range(3):
                    vstep(v, v.reduce_sum(out=s_wsc[0:1, w:w + 1], in_=s_wrow[0:1, w * 128:(w + 1) * 128],
                                 axis=AX.X))
                vstep(v, v.tensor_scalar(out=s_wsc[0:1, 3:6], in0=s_wsc[0:1, 0:3], scalar1=INV_NW,
                                scalar2=EPS_Q, op0=ALU.mult, op1=ALU.max))
                vstep(v, v.tensor_scalar(out=s_wsc[0:1, 0:3], in0=s_wsc[0:1, 3:6], scalar1=0.5,
                                scalar2=None, op0=ALU.mult))
                inc(c_v, v.tensor_scalar(out=s_wsc[0:1, 6:9], in0=s_wsc[0:1, 0:3], scalar1=-1.0,
                                scalar2=None, op0=ALU.mult))
            vn = c_v.n

            @Sy
            def _(q):
                q.wait_ge(c_v.sem, vn)
                inc(m_row, q.dma_start(out=sc_hbm, in_=s_wsc[:]), 16)
                q.wait_ge(m_row.sem, m_row.n)
                inc(m_row, q.dma_start(out=s_wscb[:], in_=sc_hbm.partition_broadcast(128)[:, 0, :]), 16)
            return m_row.n

        def affine_into_xT(tm_nm, off_idx, twait):
            @V
            def _(v):
                v.wait_ge(m_tr.sem, ln1T_done)
                v.wait_ge(c_v.sem, off_done)
                v.wait_ge(c_t.sem, twait)
                last = None
                for ct in range(NCT):
                    last = v.tensor_scalar(
                        out=xT[:, ct, :], in0=ln1T[:, ct, :],
                        scalar1=s_cv[tm_nm][:, ct:ct + 1],
                        scalar2=s_off[:, off_idx * NCT + ct:off_idx * NCT + ct + 1],
                        op0=ALU.mult, op1=ALU.add)
                inc(c_v, last)
            return c_v.n

        wslot_free = [[0], [0]]
        pg_free = [[0, "s"], [0, "s"]]
        rt_cons = [0, 0]

        def attn_pass(wsrc, mode, rhs_buf, rhs_ready_vn):
            for g in range(NCT):
                sl = g % 2
                emit_scale(3)

                @G
                def _(q, g=g, sl=sl):
                    q.wait_ge(c_t.sem, wslot_free[sl][0])
                    inc(m_w[sl], q.dma_start(out=wstrA[sl][:],
                                             in_=wsrc[g * 128:(g + 1) * 128, :]), 16)
                wn = m_w[sl].n

                if mode == "k0":
                    k0_pv = c_v.n

                    @Tn
                    def _(t, g=g, sl=sl, wn=wn, k0_pv=k0_pv):
                        t.wait_ge(m_w[sl].sem, wn)
                        t.wait_ge(c_v.sem, rhs_ready_vn)
                        if g > 0:
                            t.wait_ge(c_v.sem, k0_pv)
                        last = None
                        for ct in range(NCT):
                            last = t.matmul(p_z[0:1, 0:128], s_xk0[:, ct:ct + 1],
                                            wstrA[sl][:, ct * 128:(ct + 1) * 128],
                                            start=(ct == 0), stop=(ct == NCT - 1))
                        inc(c_t, last)
                    wslot_free[sl][0] = c_t.n
                    tcn = c_t.n

                    @V
                    def _(v, g=g, tcn=tcn):
                        v.wait_ge(c_t.sem, tcn)
                        inc(c_v, v.tensor_copy(out=s_k0r[0:1, g * 128:(g + 1) * 128],
                                               in_=p_z[0:1, 0:128]))
                    continue

                pg = p_a if g % 2 == 0 else p_b
                evprev, evkind = pg_free[g % 2]

                @Tn
                def _(t, g=g, sl=sl, wn=wn, pg=pg, evprev=evprev, evkind=evkind):
                    t.wait_ge(m_w[sl].sem, wn)
                    t.wait_ge(c_v.sem, rhs_ready_vn)
                    if evprev > 0:
                        t.wait_ge((c_s if evkind == "s" else c_v).sem, evprev)
                    last = None
                    for ct in range(NCT):
                        for th in range(2):
                            last = t.matmul(pg[:, th * 512:(th + 1) * 512],
                                            wstrA[sl][:, ct * 128:(ct + 1) * 128],
                                            rhs_buf[:, ct, th * 512:(th + 1) * 512],
                                            start=(ct == 0), stop=(ct == NCT - 1))
                    inc(c_t, last)
                wslot_free[sl][0] = c_t.n
                tcn = c_t.n

                if mode == "v":
                    @Sc
                    def _(s, g=g, pg=pg, tcn=tcn):
                        s.wait_ge(c_t.sem, tcn)
                        inc(c_s, s.activation(out=vbf[:, g, :], in_=pg[:], func=AF.Copy))
                    pg_free[g % 2] = [c_s.n, "s"]
                elif mode == "r":
                    @Sc
                    def _(s, g=g, sl=sl, pg=pg, tcn=tcn, rc=rt_cons[sl]):
                        s.wait_ge(c_t.sem, tcn)
                        if rc > 0:
                            s.wait_ge(c_v.sem, rc)
                        inc(c_s, s.activation(out=rt[sl][:], in_=pg[:], func=AF.Sigmoid))
                    pg_free[g % 2] = [c_s.n, "s"]
                    scn = c_s.n

                    @V
                    def _(v, g=g, sl=sl, scn=scn):
                        v.wait_ge(c_s.sem, scn)
                        v.tensor_copy(out=s_r0[:, g:g + 1], in_=rt[sl][:, 0:1])
                        inc(c_v, v.tensor_tensor(out=vbf[:, g, :], in0=rt[sl][:, :],
                                                 in1=vbf[:, g, :], op=ALU.mult))
                    rt_cons[sl] = c_v.n
                else:
                    @Sc
                    def _(s, g=g, pg=pg, tcn=tcn):
                        s.wait_ge(c_t.sem, tcn)
                        inc(c_s, s.activation(out=aobf[:, g, :], in_=pg[:], func=AF.Copy))
                    pg_free[g % 2] = [c_s.n, "s"]

        xv_n = affine_into_xT("tmv", 1, 0)
        attn_pass(w_at["wv"], "v", xT, xv_n)
        v_done_s = c_s.n

        @V
        def _(v):
            v.wait_ge(c_s.sem, v_done_s)
            inc(c_v, v.tensor_copy(out=s_v0[:],
                                   in_=vbf[:, :, 0:1].rearrange("p a b -> p (a b)")))
        v0_n = c_v.n

        @V
        def _(v):
            v.wait_ge(m_tr.sem, ln1T_done)
            v.wait_ge(c_v.sem, off_done)
            t = s_wkv[:, 0:NCT]
            vstep(v, v.tensor_tensor(out=t, in0=ln1T[:, :, 0:1].rearrange("p a b -> p (a b)"),
                            in1=s_cv["tmk"][:], op=ALU.mult))
            inc(c_v, v.tensor_tensor(out=s_xk0[:], in0=t, in1=s_off[:, 0:NCT], op=ALU.add))
        xk0_n = c_v.n

        @Sc
        def _(s):
            s.preload_act_table(AF.Sigmoid)

        xr_n = affine_into_xT("tmr", 2, c_t.n)
        attn_pass(w_at["wr"], "r", xT, xr_n)
        r_done_v = c_v.n

        attn_pass(w_at["wk"], "k0", None, xk0_n)
        k0_done_v = c_v.n

        @Sy
        def _(q):
            q.wait_ge(c_v.sem, k0_done_v)
            inc(m_row, q.dma_start(out=row_hbm.rearrange("a b -> (a b)").rearrange("(a b) -> a b", a=1), in_=s_k0r[:]), 16)
            q.wait_ge(m_row.sem, m_row.n)
            inc(m_row, q.dma_start(out=s_wkv[:, NCT:2 * NCT], in_=row_hbm.rearrange("a b -> b a"), allow_slow_non_contiguous=True), 16)
        k0c_n = m_row.n

        @V
        def _(v):
            v.wait_ge(m_row.sem, k0c_n)
            v.wait_ge(c_v.sem, r_done_v)
            k0c = s_wkv[:, NCT:2 * NCT]
            ww = s_wkv[:, 2 * NCT:3 * NCT]
            qq = s_wkv[:, 3 * NCT:4 * NCT]
            d1 = s_wkv[:, 4 * NCT:5 * NCT]
            d2 = s_wkv[:, 5 * NCT:6 * NCT]
            vstep(v, v.tensor_tensor(out=ww, in0=s_cv["tfc"][:], in1=k0c, op=ALU.add))
            vstep(v, v.tensor_tensor(out=qq, in0=s_cv["ppc"][:], in1=ww, op=ALU.max))
            vstep(v, v.tensor_tensor(out=d1, in0=s_cv["ppc"][:], in1=qq, op=ALU.subtract))
            inc(c_v, v.tensor_tensor(out=d2, in0=ww, in1=qq, op=ALU.subtract))
        vn = c_v.n

        @Sc
        def _(s):
            s.preload_act_table(AF.Exp)
            s.wait_ge(c_v.sem, vn)
            s.activation(out=s_wkv[:, 6 * NCT:7 * NCT], in_=s_wkv[:, 4 * NCT:5 * NCT], func=AF.Exp)
            inc(c_s, s.activation(out=s_wkv[:, 7 * NCT:8 * NCT], in_=s_wkv[:, 5 * NCT:6 * NCT],
                                  func=AF.Exp))
        sn = c_s.n

        @V
        def _(v):
            v.wait_ge(c_s.sem, sn)
            e1 = s_wkv[:, 6 * NCT:7 * NCT]
            e2 = s_wkv[:, 7 * NCT:8 * NCT]
            num = s_wkv[:, 8 * NCT:9 * NCT]
            den = s_wkv[:, 9 * NCT:10 * NCT]
            tt = s_wkv[:, 10 * NCT:11 * NCT]
            sel = s_wkv[:, 11 * NCT:12 * NCT]
            vstep(v, v.tensor_tensor(out=num, in0=e1, in1=s_cv["aac"][:], op=ALU.mult))
            vstep(v, v.tensor_tensor(out=tt, in0=e2, in1=s_v0[:], op=ALU.mult))
            vstep(v, v.tensor_tensor(out=num, in0=num, in1=tt, op=ALU.add))
            vstep(v, v.tensor_tensor(out=den, in0=e1, in1=s_cv["bbc"][:], op=ALU.mult))
            vstep(v, v.tensor_tensor(out=den, in0=den, in1=e2, op=ALU.add))
            vstep(v, v.reciprocal(out=den, in_=den))
            vstep(v, v.tensor_tensor(out=tt, in0=num, in1=den, op=ALU.mult))
            vstep(v, v.tensor_tensor(out=tt, in0=tt, in1=s_v0[:], op=ALU.subtract))
            vstep(v, v.tensor_scalar(out=tt, in0=tt, scalar1=s_flag[:, 0:1], scalar2=None, op0=ALU.mult))
            vstep(v, v.tensor_tensor(out=sel, in0=s_v0[:], in1=tt, op=ALU.add))
            vstep(v, v.tensor_tensor(out=sel, in0=s_r0[:], in1=sel, op=ALU.mult))
            inc(c_v, v.tensor_copy(out=vbf[:, :, 0:1].rearrange("p a b -> p (a b)"), in_=sel))
        a0_n = c_v.n
        maybe_stop(3)

        join()
        lctx.close(); open_stacks.remove(lctx)
        a2ctx = ExitStack()
        open_stacks.append(a2ctx)
        aobf = a2ctx.enter_context(nc.sbuf_tensor("aobf", [128, NCT, T], BF16))
        ao_tok = a2ctx.enter_context(nc.sbuf_tensor("ao_tok", [128, NT, C], BF16))
        maybe_stop(30)

        if KDBG:
            @G
            def _(q):
                q.wait_ge(c_v.sem, a0_n)
                inc(m_dbg, q.dma_start(out=dbg_vb, in_=vbf.ap().rearrange("p a b -> p (a b)")), 16)
        attn_pass(w_at["wo"], "o", vbf, a0_n)
        ao_done_s = c_s.n
        emit_scale(999)
        wscb_done = scale_finalize()
        maybe_stop(31)

        @Sc
        def _(s):
            s.wait_ge(c_s.sem, ao_done_s)
            for m in range(NCT):
                issue_transpose(s, m_tr, ao_tok[:, :, m * 128:(m + 1) * 128],
                                aobf[:, m, :].rearrange("p (a b) -> p a b", a=NT))
        aot_n = m_tr.n
        maybe_stop(32)

        @Sc
        def _(s):
            s.preload_act_table(AF.Sqrt)

        for j in range(NT):
            @Sc
            def _(s, j=j):
                s.wait_ge(m_tr.sem, aot_n)
                inc(c_s, s.activation(out=dump[:, 0:C], in_=ao_tok[:, j, :], func=AF.Square,
                                      accum_out=s_stat[:, 8 + j:9 + j]))

            @V
            def _(v, j=j):
                v.wait_ge(m_tr.sem, aot_n)
                inc(c_v, v.reduce_sum(out=s_stat[:, j:j + 1], in_=ao_tok[:, j, :], axis=AX.X))

        ln_math(EPS_LN)
        lna_rs = c_v.n
        if KDBG:
            @G
            def _(q):
                q.wait_ge(c_v.sem, lna_rs)
                inc(m_dbg, q.dma_start(out=dbg_ao, in_=ao_tok.ap().rearrange("p a b -> p (a b)")), 16)
                inc(m_dbg, q.dma_start(out=dbg_st, in_=s_stat[:]), 16)
        maybe_stop(33)

        for j in range(NT):
            sl = j % 2

            @Sy
            def _(q, j=j, sl=sl):
                q.wait_ge(c_v.sem, c_v.n)
                if j >= 2:
                    q.wait_ge(m_x1w.sem, m_x1w.n)
                inc(m_ld[sl], q.dma_start(out=xw[sl][:], in_=x_in[j * 128:(j + 1) * 128, :]), 16)
            ldn = m_ld[sl].n

            @V
            def _(v, j=j, sl=sl, ldn=ldn):
                v.wait_ge(m_ld[sl].sem, ldn)
                v.wait_ge(c_v.sem, lna_rs)
                vstep(v, v.tensor_scalar(out=x1s[sl][:], in0=ao_tok[:, j, :],
                                scalar1=s_stat[:, 16 + j:17 + j],
                                scalar2=s_stat[:, 24 + j:25 + j],
                                op0=ALU.subtract, op1=ALU.mult))
                vstep(v, v.tensor_tensor(out=x1s[sl][:], in0=x1s[sl][:], in1=xw[sl][:], op=ALU.add))
                v.reduce_sum(out=s_qcol[:, 48 + j:49 + j], in_=x1s[sl][:], axis=AX.X)
                v.reduce_max(out=s_qcol[:, j:j + 1], in_=x1s[sl][:], axis=AX.X)
                inc(c_v, v.tensor_reduce(out=s_qcol[:, 8 + j:9 + j], in_=x1s[sl][:],
                                         axis=AX.X, op=ALU.min))
            vvn = c_v.n

            @Sc
            def _(s, j=j, sl=sl, vvn=vvn):
                s.wait_ge(c_v.sem, vvn)
                inc(c_s, s.activation(out=dump[:, 0:C], in_=x1s[sl][:], func=AF.Square,
                                      accum_out=s_stat[:, 8 + j:9 + j]))

            @G
            def _(q, j=j, sl=sl, vvn=vvn):
                q.wait_ge(c_v.sem, vvn)
                inc(m_x1w, q.dma_start(out=x1_hbm[j * 128:(j + 1) * 128, :], in_=x1s[sl][:]), 16)

        maybe_stop(4)

        @V
        def _(v):
            v.wait_ge(c_s.sem, c_s.n)
            v.wait_ge(c_v.sem, c_v.n)
            inc(c_v, v.tensor_copy(out=s_stat[:, 0:8], in_=s_qcol[:, 48:56]))

        ln_math(EPS_LN)

        @V
        def _(v):
            v.wait_ge(m_row.sem, wscb_done)
            v.wait_ge(c_v.sem, c_v.n)
            mean = s_stat[:, 16:24]
            rs = s_stat[:, 24:32]
            t = s_qcol[:, 16:24]
            u = s_qcol[:, 24:32]
            vstep(v, v.tensor_tensor(out=t, in0=s_qcol[:, 0:8], in1=mean, op=ALU.subtract))
            vstep(v, v.tensor_tensor(out=u, in0=mean, in1=s_qcol[:, 8:16], op=ALU.subtract))
            vstep(v, v.tensor_tensor(out=t, in0=t, in1=u, op=ALU.max))
            vstep(v, v.tensor_tensor(out=t, in0=t, in1=rs, op=ALU.mult))
            vstep(v, v.tensor_scalar(out=t, in0=t, scalar1=EPS_Q, scalar2=INV_128,
                            op0=ALU.max, op1=ALU.mult))
            vstep(v, v.reciprocal(out=u, in_=t))
            vstep(v, v.tensor_tensor(out=s_qcol[:, 32:40], in0=rs, in1=u, op=ALU.mult))
            vstep(v, v.tensor_scalar(out=s_qcol[:, 40:48], in0=t, scalar1=s_wscb[:, 3:4],
                            scalar2=None, op0=ALU.mult))
            vstep(v, v.tensor_scalar(out=s_qcol[:, 48:56], in0=t, scalar1=s_wscb[:, 4:5],
                                     scalar2=None, op0=ALU.mult))
            vstep(v, v.reciprocal(out=s_qcol[:, 56:64], in_=s_qcol[:, 32:40]))
            vstep(v, v.tensor_scalar(out=s_qcol[:, 56:64], in0=s_qcol[:, 56:64], scalar1=-0.5,
                                     scalar2=None, op0=ALU.mult))
            inc(c_v, v.tensor_tensor(out=s_qcol[:, 56:64], in0=s_qcol[:, 56:64],
                                     in1=s_stat[:, 16:24], op=ALU.add))
        qcol_n = c_v.n

        join()
        a2ctx.close(); open_stacks.remove(a2ctx)
        actx.close(); open_stacks.remove(actx)

        bctx = ExitStack()
        open_stacks.append(bctx)
        be = bctx.enter_context
        qT = be(nc.sbuf_tensor("qT", [128, NCT, T], BF16))
        q_tok = [be(nc.sbuf_tensor(f"qtok{i}", [128, C], BF16)) for i in range(2)]
        tern_i = [be(nc.sbuf_tensor(f"terni{i}", [128, C], I32)) for i in range(2)]
        tcm = [be(nc.sbuf_tensor(f"tcm{i}", [128, C], F32)) for i in range(2)]
        ternbf = [be(nc.sbuf_tensor(f"ternbf{i}", [128, C], BF16)) for i in range(2)]
        tcb = [be(nc.sbuf_tensor(f"tcb{i}", [128, C], BF16)) for i in range(2)]
        wstr = [be(nc.sbuf_tensor(f"wstr{i}", [128, C], BF16)) for i in range(2)]
        csg_b = be(nc.sbuf_tensor("csg_b", [128, T], F32))
        gt = be(nc.sbuf_tensor("gt", [128, T], F32))
        gs = be(nc.sbuf_tensor("gs", [128, T], F32))
        itile = be(nc.sbuf_tensor("itile", [128, T], BF16))
        zsb = [be(nc.sbuf_tensor(f"zsb{i}", [128, T], BF16)) for i in range(2)]

        @Sy
        def _(q):
            q.wait_ge(c_v.sem, qcol_n)
            inc(m_row, q.dma_start(out=row_hbm[0:8, :].rearrange("a b -> b a"),
                                   in_=s_qcol[:, 40:48], allow_slow_non_contiguous=True), 16)
            q.wait_ge(m_row.sem, m_row.n)
            inc(m_row, q.dma_start(out=csg_b[:], in_=row_hbm.rearrange("a b -> (a b)")[0:T]
                                   .rearrange("(a b) -> a b", a=1).partition_broadcast(128)[:, 0, :]), 16)
        cs_b_n = m_row.n

        p5x = [0, 0]
        for j in range(NT):
            sl = j % 2

            @Sy
            def _(q, j=j, sl=sl, px=p5x[sl]):
                q.wait_ge(m_x1w.sem, m_x1w.n)
                if j < 2:
                    q.wait_ge(c_v.sem, c_v.n)
                else:
                    q.wait_ge(c_v.sem, px)
                inc(m_ld[sl], q.dma_start(out=x1s[sl][:], in_=x1_hbm[j * 128:(j + 1) * 128, :]), 16)
            ldn = m_ld[sl].n

            mtr_prev = m_tr.n

            @V
            def _(v, j=j, sl=sl, ldn=ldn, mtr_prev=mtr_prev):
                v.wait_ge(m_ld[sl].sem, ldn)
                if j >= 2:
                    v.wait_ge(m_tr.sem, mtr_prev)
                vstep(v, v.tensor_scalar(out=xw[sl][:], in0=x1s[sl][:],
                                scalar1=s_qcol[:, 56 + j:57 + j],
                                scalar2=s_qcol[:, 32 + j:33 + j],
                                op0=ALU.subtract, op1=ALU.mult))
                vstep(v, v.tensor_scalar(out=xw[sl][:], in0=xw[sl][:], scalar1=127.5,
                                scalar2=-127.5, op0=ALU.min, op1=ALU.max))
                vstep(v, v.tensor_copy(out=tern_i[sl][:], in_=xw[sl][:]))
                vstep(v, v.tensor_tensor(out=tcm[sl][:], in0=tern_i[sl][:], in1=xw[sl][:],
                                         op=ALU.is_gt))
                inc(c_v, v.tensor_tensor(out=q_tok[sl][:], in0=tern_i[sl][:],
                                         in1=tcm[sl][:], op=ALU.subtract))
            vvn = c_v.n
            p5x[sl] = vvn

            mtr_all = m_tr.n

            @Sc
            def _(s, j=j, sl=sl, vvn=vvn, mtr_all=mtr_all):
                s.wait_ge(c_v.sem, vvn)
                s.wait_ge(m_tr.sem, mtr_all)
                issue_transpose(s, m_tr, qT[:, :, j * 128:(j + 1) * 128],
                                q_tok[sl].ap().rearrange("p (k f) -> p k f", k=NCT))

            if KDBG:
                @G
                def _(q, j=j, sl=sl, vvn=vvn):
                    q.wait_ge(c_v.sem, vvn)
                    inc(m_dbg, q.dma_start(out=dbg_q[j * 128:(j + 1) * 128, :], in_=q_tok[sl][:]), 16)
        qT_n = m_tr.n
        maybe_stop(5)

        @Sc
        def _(s):
            s.preload_act_table(AF.Silu)

        tern_done = [0, 0]
        mm_done = [0, 0]
        ps_free = [0, 0]
        it_cons = 0
        zev = [0, 0]
        for g in range(NG):
            for wi, (wsrc, widx) in enumerate(((wg, 0), (wu, 1))):
                sl = wi

                @G
                def _(q, g=g, sl=sl, wsrc=wsrc, td=tern_done[sl]):
                    if td > 0:
                        q.wait_ge(c_v.sem, td)
                    inc(m_fw[sl], q.dma_start(out=wstr[sl][:],
                                              in_=wsrc[g * 128:(g + 1) * 128, :]), 16)
                ldn = m_fw[sl].n

                @V
                def _(v, sl=sl, ldn=ldn, widx=widx, md=mm_done[sl]):
                    v.wait_ge(m_fw[sl].sem, ldn)
                    v.wait_ge(m_row.sem, wscb_done)
                    if md > 0:
                        v.wait_ge(c_t.sem, md)
                    vstep(v, v.tensor_scalar(out=tcb[sl][:], in0=wstr[sl][:],
                                    scalar1=s_wscb[:, widx:widx + 1], scalar2=None,
                                    op0=ALU.is_ge))
                    vstep(v, v.tensor_scalar(out=wstr[sl][:], in0=wstr[sl][:],
                                    scalar1=s_wscb[:, 6 + widx:7 + widx], scalar2=None,
                                    op0=ALU.is_le))
                    inc(c_v, v.tensor_tensor(out=ternbf[sl][:], in0=tcb[sl][:], in1=wstr[sl][:],
                                             op=ALU.subtract))
                tern_done[sl] = c_v.n
                vvn = c_v.n

                pq = p_a if wi == 0 else p_b

                @Tn
                def _(t, sl=sl, vvn=vvn, pq=pq, g=g, psf=ps_free[wi]):
                    t.wait_ge(c_v.sem, vvn)
                    t.wait_ge(m_tr.sem, qT_n)
                    if psf > 0:
                        t.wait_ge(c_v.sem, psf)
                    last = None
                    for ct in range(NCT):
                        for th in range(2):
                            last = t.matmul(pq[:, th * 512:(th + 1) * 512],
                                            ternbf[sl][:, ct * 128:(ct + 1) * 128],
                                            qT[:, ct, th * 512:(th + 1) * 512],
                                            start=(ct == 0), stop=(ct == NCT - 1))
                    inc(c_t, last)
                mm_done[sl] = c_t.n

            tcn = c_t.n
            zsl = g % 2
            pz = p_z if g % 2 == 0 else p_z2

            @V
            def _(v, g=g, tcn=tcn):
                v.wait_ge(c_t.sem, tcn)
                v.wait_ge(c_v.sem, c_v.n)
                v.wait_ge(c_s.sem, c_s.n)
                inc(c_v, v.tensor_tensor(out=gt[:], in0=p_a[:], in1=csg_b[:], op=ALU.mult))
            ps_free[0] = c_v.n
            gt_n = c_v.n

            @Sc
            def _(s, g=g, gt_n=gt_n):
                s.wait_ge(c_v.sem, gt_n)
                inc(c_s, s.activation(out=gs[:], in_=gt[:], func=AF.Silu))
            ssn = c_s.n

            @V
            def _(v, g=g, ssn=ssn, itc=it_cons):
                v.wait_ge(c_s.sem, ssn)
                if itc > 0:
                    v.wait_ge(c_t.sem, itc)
                inc(c_v, v.tensor_tensor(out=itile[:], in0=gs[:], in1=p_b[:], op=ALU.mult))
            ps_free[1] = c_v.n
            ivn = c_v.n
            if KDBG and g == 0:
                @G
                def _(q, ivn=ivn):
                    q.wait_ge(c_v.sem, ivn)
                    inc(m_dbg, q.dma_start(out=dbg_g, in_=gt[:]), 16)
                    inc(m_dbg, q.dma_start(out=dbg_i, in_=itile[:]), 16)

            @Tn
            def _(t, g=g, ivn=ivn, pz=pz, zv=zev[g % 2]):
                t.wait_ge(c_v.sem, ivn)
                if zv > 0:
                    t.wait_ge(c_s.sem, zv)
                last = None
                for th in range(2):
                    last = t.matmul(pz[:, th * 512:(th + 1) * 512], s_h128[:],
                                    itile[:, th * 512:(th + 1) * 512],
                                    start=True, stop=True)
                inc(c_t, last)
            it_cons = c_t.n
            tzn = c_t.n

            @Sc
            def _(s, g=g, zsl=zsl, tzn=tzn, pz=pz):
                s.wait_ge(c_t.sem, tzn)
                if g >= 2:
                    s.wait_ge(m_zw.sem, m_zw.n)
                inc(c_s, s.activation(out=zsb[zsl][:], in_=pz[:], func=AF.Copy))
            zev[g % 2] = c_s.n
            szn = c_s.n

            @Sy
            def _(q, g=g, zsl=zsl, szn=szn):
                q.wait_ge(c_s.sem, szn)
                inc(m_zw, q.dma_start(
                    out=z_hbm.rearrange("(a b) t -> a b t", b=NG)[:, g, :],
                    in_=zsb[zsl][:]), 16)
        z_all = m_zw.n
        maybe_stop(6)

        join()
        bctx.close(); open_stacks.remove(bctx)

        cdctx = ExitStack()
        open_stacks.append(cdctx)
        ce = cdctx.enter_context
        s2_b = ce(nc.sbuf_tensor("s2_b", [128, T], F32))
        cs2d_b = ce(nc.sbuf_tensor("cs2d_b", [128, T], F32))
        cctx = ExitStack()
        open_stacks.append(cctx)
        cce = cctx.enter_context
        zpr = [cce(nc.sbuf_tensor(f"zpr{i}", [128, 2 * T], BF16)) for i in range(2)]
        hbf = [cce(nc.sbuf_tensor(f"hbf{i}", [128, 2 * T], BF16)) for i in range(2)]
        ysc = cce(nc.sbuf_tensor("ysc", [128, 2 * T], F32))
        yin = cce(nc.sbuf_tensor("yin", [128, 2 * T], I32))
        mctx = ExitStack()
        open_stacks.append(mctx)
        mxa = mctx.enter_context(nc.sbuf_tensor("mxa", [128, T], BF16))
        mxT = mctx.enter_context(nc.sbuf_tensor("mxT", [128, NT, 128], BF16))

        z2 = z_hbm

        zslot = [0, 0]
        pzAB = [None, None]
        hb_cons = [0, 0]
        for u in range(NG // 2):
            sl = u % 2

            @Sy
            def _(q, u=u, sl=sl, zs=zslot[sl]):
                q.wait_ge(m_zw.sem, z_all)
                if zs > 0:
                    q.wait_ge(c_t.sem, zs)
                inc(m_z[sl], q.dma_start(
                    out=zpr[sl].ap().rearrange("p (a t) -> p a t", a=2),
                    in_=z2[u * 256:(u + 1) * 256, :].rearrange("(a p) t -> p a t", p=128)), 16)
            ldn = m_z[sl].n

            @Tn
            def _(t, u=u, sl=sl, ldn=ldn, pva=pzAB[0], pvb=pzAB[1]):
                t.wait_ge(m_z[sl].sem, ldn)
                for pv in (pva, pvb):
                    if pv is not None:
                        t.wait_ge(pv[0].sem, pv[1])
                last = None
                for half, pz in ((0, p_z), (1, p_z2)):
                    for th in range(2):
                        o = half * 1024 + th * 512
                        last = t.matmul(pz[:, th * 512:(th + 1) * 512], s_h64b[:],
                                        zpr[sl][:, o:o + 512], start=True, stop=True)
                inc(c_t, last)
            zslot[sl] = c_t.n
            tcn = c_t.n

            @Sc
            def _(s, u=u, sl=sl, tcn=tcn, hv=hb_cons[sl]):
                s.wait_ge(c_t.sem, tcn)
                if hv > 0:
                    s.wait_ge(c_v.sem, hv)
                s.activation(out=hbf[sl][:, 0:T], in_=p_z[:], func=AF.Abs)
                inc(c_s, s.activation(out=hbf[sl][:, T:2 * T], in_=p_z2[:], func=AF.Abs))
            pzAB[0] = (c_s, c_s.n)
            pzAB[1] = (c_s, c_s.n)
            ssn = c_s.n

            @V
            def _(v, u=u, sl=sl, ssn=ssn):
                v.wait_ge(c_s.sem, ssn)
                v.wait_ge(c_v.sem, c_v.n)
                vstep(v, v.tensor_tensor(out=hbf[sl][:, 0:T], in0=hbf[sl][:, 0:T],
                                         in1=hbf[sl][:, T:2 * T], op=ALU.max))
                if u == 0:
                    inc(c_v, v.tensor_copy(out=mxa[:], in_=hbf[sl][:, 0:T]))
                else:
                    inc(c_v, v.tensor_tensor(out=mxa[:], in0=hbf[sl][:, 0:T], in1=mxa[:],
                                             op=ALU.max))
            hb_cons[sl] = c_v.n
        maxn = c_v.n

        @Sc
        def _(s):
            s.wait_ge(c_v.sem, maxn)
            issue_transpose(s, m_tr, mxT[:],
                            mxa.ap().rearrange("p (k f) -> p k f", k=NT))
        trn = m_tr.n

        @V
        def _(v):
            v.wait_ge(m_tr.sem, trn)
            vstep(v, v.reduce_max(out=s_qcol[:, 0:8], in_=mxT[:], axis=AX.X))
            vstep(v, v.tensor_scalar(out=s_qcol[:, 8:16], in0=s_qcol[:, 0:8], scalar1=EPS_Q,
                            scalar2=INV_128, op0=ALU.max, op1=ALU.mult))
            vstep(v, v.reciprocal(out=s_qcol[:, 16:24], in_=s_qcol[:, 8:16]))
            vstep(v, v.tensor_scalar(out=s_qcol[:, 24:32], in0=s_qcol[:, 8:16],
                                     scalar1=s_wscb[:, 5:6], scalar2=None, op0=ALU.mult))
            inc(c_v, v.tensor_tensor(out=s_qcol[:, 24:32], in0=s_qcol[:, 24:32],
                                     in1=s_qcol[:, 48:56], op=ALU.mult))
        s2n = c_v.n

        @Sy
        def _(q):
            q.wait_ge(c_v.sem, s2n)
            inc(m_row, q.dma_start(out=row_hbm[0:8, :].rearrange("a b -> b a"),
                                   in_=s_qcol[:, 16:24], allow_slow_non_contiguous=True), 16)
            q.wait_ge(m_row.sem, m_row.n)
            inc(m_row, q.dma_start(out=s2_b[:], in_=row_hbm.rearrange("a b -> (a b)")[0:T]
                                   .rearrange("(a b) -> a b", a=1).partition_broadcast(128)[:, 0, :]), 16)
            q.wait_ge(m_row.sem, m_row.n)
            inc(m_row, q.dma_start(out=row_hbm[0:8, :].rearrange("a b -> b a"),
                                   in_=s_qcol[:, 24:32], allow_slow_non_contiguous=True), 16)
            q.wait_ge(m_row.sem, m_row.n)
            inc(m_row, q.dma_start(out=cs2d_b[:], in_=row_hbm.rearrange("a b -> (a b)")[0:T]
                                   .rearrange("(a b) -> a b", a=1).partition_broadcast(128)[:, 0, :]), 16)
        s2b_n = m_row.n
        mctx.close(); open_stacks.remove(mctx)
        q2s = [cce(nc.sbuf_tensor(f"q2s{i}", [128, 2 * T], BF16)) for i in range(2)]
        hg = cce(nc.sbuf_tensor("hg", [128, 2 * T], F32))
        maybe_stop(7)

        for u in range(NG // 2):
            sl = u % 2

            @Sy
            def _(q, u=u, sl=sl, zs=zslot[sl]):
                q.wait_ge(m_row.sem, s2b_n)
                if zs > 0:
                    q.wait_ge(c_t.sem, zs)
                inc(m_z[sl], q.dma_start(
                    out=zpr[sl].ap().rearrange("p (a t) -> p a t", a=2),
                    in_=z2[u * 256:(u + 1) * 256, :].rearrange("(a p) t -> p a t", p=128)), 16)
            ldn = m_z[sl].n

            @Tn
            def _(t, u=u, sl=sl, ldn=ldn, pva=pzAB[0], pvb=pzAB[1]):
                t.wait_ge(m_z[sl].sem, ldn)
                for pv in (pva, pvb):
                    if pv is not None:
                        t.wait_ge(pv[0].sem, pv[1])
                last = None
                for half, pz in ((0, p_z), (1, p_z2)):
                    for th in range(2):
                        o = half * 1024 + th * 512
                        last = t.matmul(pz[:, th * 512:(th + 1) * 512], s_h64b[:],
                                        zpr[sl][:, o:o + 512], start=True, stop=True)
                inc(c_t, last)
            zslot[sl] = c_t.n
            tcn = c_t.n

            @V
            def _(v, u=u, sl=sl, tcn=tcn, q2w=m_q2w.n):
                v.wait_ge(c_t.sem, tcn)
                v.wait_ge(c_v.sem, c_v.n)
                if u >= 2:
                    v.wait_ge(m_q2w.sem, q2w)
                vstep(v, v.tensor_tensor(out=ysc[:, 0:T], in0=p_z[:], in1=s2_b[:],
                                         op=ALU.mult))
                vstep(v, v.tensor_tensor(out=ysc[:, T:2 * T], in0=p_z2[:], in1=s2_b[:],
                                         op=ALU.mult))
                vstep(v, v.tensor_scalar(out=ysc[:], in0=ysc[:], scalar1=0.5, scalar2=127.5,
                                         op0=ALU.add, op1=ALU.min))
                vstep(v, v.tensor_scalar(out=ysc[:], in0=ysc[:], scalar1=-127.5, scalar2=None,
                                         op0=ALU.max))
                vstep(v, v.tensor_copy(out=yin[:], in_=ysc[:]))
                vstep(v, v.tensor_tensor(out=hg[:], in0=yin[:], in1=ysc[:], op=ALU.is_gt))
                inc(c_v, v.tensor_tensor(out=q2s[sl][:], in0=yin[:], in1=hg[:],
                                         op=ALU.subtract))
            pzAB[0] = (c_v, c_v.n - 6)
            pzAB[1] = (c_v, c_v.n - 5)
            qvn = c_v.n

            @G
            def _(q, u=u, sl=sl, qvn=qvn):
                q.wait_ge(c_v.sem, qvn)
                qv = q2_hbm.rearrange("(j1 j2) t -> j1 j2 t", j2=128)
                for half in range(2):
                    jp = 2 * u + half
                    inc(m_q2w, q.dma_start(out=qv[:, 2 * jp, :],
                                           in_=q2s[sl][0:64, half * T:(half + 1) * T]), 16)
                    inc(m_q2w, q.dma_start(out=qv[:, 2 * jp + 1, :],
                                           in_=q2s[sl][64:128, half * T:(half + 1) * T]), 16)
        q2_all = m_q2w.n
        maybe_stop(8)

        join()
        cctx.close(); open_stacks.remove(cctx)
        q2r = ce(nc.sbuf_tensor("q2r", [128, NG, T], BF16))

        @Sy
        def _(q):
            q.wait_ge(m_q2w.sem, q2_all)
            for k in range(NG):
                inc(m_q2r, q.dma_start(out=q2r[:, k, :], in_=q2_hbm[k * 128:(k + 1) * 128, :]), 16)
        q2r_n = m_q2r.n

        CHK = 1024
        NKC = 8
        wstrD = [ce(nc.sbuf_tensor(f"wstrD{i}", [128, CHK], BF16)) for i in range(2)]
        t1D = [ce(nc.sbuf_tensor(f"t1D{i}", [128, CHK], BF16)) for i in range(2)]
        ternbD = [ce(nc.sbuf_tensor(f"ternbD{i}", [128, CHK], BF16)) for i in range(2)]
        dnbf = [ce(nc.sbuf_tensor(f"dnbf{i}", [128, T], BF16)) for i in range(2)]

        ternD_done = [0, 0]
        mmD_done = [0, 0]
        dps_free = [0, 0]
        for m in range(NCT):
            pq = p_a if m % 2 == 0 else p_b
            for qx in range(8):
                i = 8 * m + qx
                sl = i % 2

                @G
                def _(q, m=m, qx=qx, sl=sl, td=ternD_done[sl]):
                    if td > 0:
                        q.wait_ge(c_v.sem, td)
                    inc(m_fw[sl], q.dma_start(
                        out=wstrD[sl][:],
                        in_=wd[m * 128:(m + 1) * 128, qx * 1024:(qx + 1) * 1024]), 16)
                ldn = m_fw[sl].n

                @V
                def _(v, sl=sl, ldn=ldn, md=mmD_done[sl]):
                    v.wait_ge(m_fw[sl].sem, ldn)
                    if md > 0:
                        v.wait_ge(c_t.sem, md)
                    vstep(v, v.tensor_scalar(out=t1D[sl][:], in0=wstrD[sl][:],
                                    scalar1=s_wscb[:, 2:3], scalar2=None, op0=ALU.is_ge))
                    vstep(v, v.tensor_scalar(out=wstrD[sl][:], in0=wstrD[sl][:],
                                    scalar1=s_wscb[:, 8:9], scalar2=None, op0=ALU.is_le))
                    inc(c_v, v.tensor_tensor(out=ternbD[sl][:], in0=t1D[sl][:], in1=wstrD[sl][:],
                                             op=ALU.subtract))
                ternD_done[sl] = c_v.n
                vvn = c_v.n

                @Tn
                def _(t, m=m, qx=qx, sl=sl, vvn=vvn, pq=pq, psf=dps_free[m % 2]):
                    t.wait_ge(c_v.sem, vvn)
                    t.wait_ge(m_q2r.sem, q2r_n)
                    if qx == 0 and psf > 0:
                        t.wait_ge(c_v.sem, psf)
                    last = None
                    for ct in range(NKC):
                        k = qx * NKC + ct
                        for th in range(2):
                            last = t.matmul(pq[:, th * 512:(th + 1) * 512],
                                            ternbD[sl][:, ct * 128:(ct + 1) * 128],
                                            q2r[:, k, th * 512:(th + 1) * 512],
                                            start=(k == 0), stop=(k == NG - 1))
                    inc(c_t, last)
                mmD_done[sl] = c_t.n

            tcn = c_t.n
            dsl = m % 2

            @V
            def _(v, m=m, dsl=dsl, tcn=tcn, pq=pq):
                v.wait_ge(c_t.sem, tcn)
                v.wait_ge(c_v.sem, c_v.n)
                if m >= 2:
                    v.wait_ge(m_dnw.sem, m_dnw.n)
                inc(c_v, v.tensor_tensor(out=dnbf[dsl][:], in0=pq[:], in1=cs2d_b[:],
                                         op=ALU.mult))
            dps_free[m % 2] = c_v.n
            dvn = c_v.n

            @G
            def _(q, m=m, dsl=dsl, dvn=dvn):
                q.wait_ge(c_v.sem, dvn)
                inc(m_dnw, q.dma_start(out=dn_hbm[m * 128:(m + 1) * 128, :], in_=dnbf[dsl][:]), 16)
        dn_all = m_dnw.n
        maybe_stop(9)

        join()
        cdctx.close(); open_stacks.remove(cdctx)

        ectx = ExitStack()
        open_stacks.append(ectx)
        ee = ectx.enter_context
        dntok = ee(nc.sbuf_tensor("dntok", [128, NT, C], BF16))
        dumpE = ee(nc.sbuf_tensor("dumpE", [128, C], F32))
        outb = [ee(nc.sbuf_tensor(f"outb{i}", [128, C], F32)) for i in range(2)]

        @Sc
        def _(s):
            s.preload_act_table(AF.Sqrt)
            s.wait_ge(m_dnw.sem, dn_all)
            for k in range(NCT):
                issue_transpose(s, m_tr, dntok[:, :, k * 128:(k + 1) * 128],
                                dn_hbm[k * 128:(k + 1) * 128, :]
                                .rearrange("p (j f) -> p j f", j=NT))
        dnt_n = m_tr.n

        for j in range(NT):
            @Sc
            def _(s, j=j):
                s.wait_ge(m_tr.sem, dnt_n)
                inc(c_s, s.activation(out=dumpE[:], in_=dntok[:, j, :], func=AF.Square,
                                      accum_out=s_stat[:, 8 + j:9 + j]))

            @V
            def _(v, j=j):
                v.wait_ge(m_tr.sem, dnt_n)
                inc(c_v, v.reduce_sum(out=s_stat[:, j:j + 1], in_=dntok[:, j, :], axis=AX.X))

        ln_math(EPS_LN)
        lnf_rs = c_v.n

        for j in range(NT):
            sl = j % 2

            @Sy
            def _(q, j=j, sl=sl):
                q.wait_ge(c_v.sem, c_v.n)
                if j >= 2:
                    q.wait_ge(m_out.sem, m_out.n)
                inc(m_ld[sl], q.dma_start(out=x1s[sl][:], in_=x1_hbm[j * 128:(j + 1) * 128, :]), 16)
            ldn = m_ld[sl].n

            @V
            def _(v, j=j, sl=sl, ldn=ldn):
                v.wait_ge(m_ld[sl].sem, ldn)
                v.wait_ge(c_v.sem, lnf_rs)
                vstep(v, v.tensor_scalar(out=outb[sl][:], in0=dntok[:, j, :],
                                scalar1=s_stat[:, 16 + j:17 + j],
                                scalar2=s_stat[:, 24 + j:25 + j],
                                op0=ALU.subtract, op1=ALU.mult))
                inc(c_v, v.tensor_tensor(out=outb[sl][:], in0=outb[sl][:], in1=x1s[sl][:],
                                         op=ALU.add))
            ovn = c_v.n

            @Sy
            def _(q, j=j, sl=sl, ovn=ovn):
                q.wait_ge(c_v.sem, ovn)
                inc(m_out, q.dma_start(out=o_out[j * 128:(j + 1) * 128, :], in_=outb[sl][:]), 16)

        @Sy
        def _(q):
            q.wait_ge(m_out.sem, m_out.n)

        ectx.close(); open_stacks.remove(ectx)
    except _Stop:
        for st in reversed(open_stacks):
            st.close()
    ctx.close()
    return nc


def _hadamard(n):
    h = np.array([[1.0]], dtype=np.float64)
    while h.shape[0] < n:
        h = np.block([[h, h], [h, -h]])
    return h


def kernel(L0_attn_x, L0_attn_time_first, L0_attn_time_mix_k, L0_attn_time_mix_v,
           L0_attn_time_mix_r, L0_attn_xx, L0_attn_aa, L0_attn_bb, L0_attn_pp,
           L0_attn_w_key, L0_attn_w_value, L0_attn_w_rec, L0_attn_w_out,
           L0_ffn_w_gate, L0_ffn_w_up, L0_ffn_w_down):
    nc = build_nc()

    def col_form(v):
        return np.ascontiguousarray(np.asarray(v, dtype=np.float32)
                                    .reshape(NCT, 128).T)

    h64 = _hadamard(64)
    h128 = np.ascontiguousarray(_hadamard(128).astype(np.float32))
    c32 = np.float32(1.0 / np.sqrt(np.float64(II)))
    h64b = np.zeros((128, 128), dtype=np.float32)
    h64b[0:64, 0:64] = (h64 * np.float64(c32)).astype(np.float32)
    h64b[64:128, 64:128] = h64b[0:64, 0:64]

    def blk_oc(w):
        o, ci = w.shape
        return np.ascontiguousarray(
            np.asarray(w, dtype=np.float32).reshape(o // 128, 128, ci // 128, 128)
            .transpose(0, 3, 2, 1).reshape(o, ci))

    shared = {
        "wk": blk_oc(L0_attn_w_key),
        "wv": blk_oc(L0_attn_w_value),
        "wr": blk_oc(L0_attn_w_rec),
        "wo": blk_oc(L0_attn_w_out),
        "wg": blk_oc(L0_ffn_w_gate),
        "wu": blk_oc(L0_ffn_w_up),
        "wd": blk_oc(L0_ffn_w_down),
        "tmk": col_form(L0_attn_time_mix_k),
        "tmv": col_form(L0_attn_time_mix_v),
        "tmr": col_form(L0_attn_time_mix_r),
        "tfc": col_form(L0_attn_time_first),
        "xxc": col_form(L0_attn_xx),
        "aac": col_form(L0_attn_aa),
        "bbc": col_form(L0_attn_bb),
        "ppc": col_form(L0_attn_pp),
        "h128": h128,
        "h64b": h64b,
    }
    xfull = np.ascontiguousarray(np.asarray(L0_attn_x, dtype=np.float32)
                                 .reshape(8192, C))
    in_maps = []
    for c in range(NCORES_G):
        m = dict(shared)
        m["x"] = np.ascontiguousarray(xfull[c * T:(c + 1) * T, :])
        m["flag"] = np.array([[1.0 if c == 0 else 0.0]], dtype=np.float32)
        in_maps.append(m)

    res = run_bass_kernel_spmd(nc, in_maps, core_ids=list(range(NCORES_G)))
    out = np.concatenate([r["out"] for r in res.results], axis=0)
    return out.reshape(1, 8192, C).astype(np.float32)

